# version 22
# baseline (speedup 1.0000x reference)
import contextlib
import numpy as np
import concourse.bass as bass
import concourse.mybir as mybir
from concourse.bass_utils import run_bass_kernel_spmd

F32 = mybir.dt.float32
BF16 = mybir.dt.bfloat16
I32 = mybir.dt.int32
I8 = mybir.dt.int8
AF = mybir.ActivationFunctionType
OP = mybir.AluOpType

D = 1024
T = 2048
SEQ = 4096
HF = 3072
LAT = 256
EPS = 1e-5
NS = D // 128
NJ = HF // 128
NP = 8
TCF = 1024
NCF = T // TCF
QCW = 512
NQC = T // QCW
C2 = 512
NC2 = T // C2
NKT = SEQ // 128

ENGINES = ("sync", "gpsimd", "scalar", "vector", "tensor")

TRUNC = None
FINE = None


class Planner:

    def __init__(self, nc, sems, ctx):
        self.nc = nc
        self.ctx = ctx
        self.sems = dict(sems)
        self.count = {k: 0 for k in self.sems}
        self.items = {e: [] for e in ENGINES}
        self.dma_owner = {}

    def _sem(self, name):
        if name not in self.sems:
            self.sems[name] = self.ctx.enter_context(self.nc.semaphore(name))
            self.count[name] = 0
        return self.sems[name]

    def op(self, eng, fn, wait=(), inc=()):
        w = [(self._sem(s), v) for s, v in wait if v > 0]
        i = []
        for s, n in inc:
            self._sem(s)
            self.count[s] += n
            i.append((self.sems[s], n))
        self.items[eng].append((w, fn, i))

    def dma(self, eng, fn, sem, wait=()):
        self._sem(sem)
        own = self.dma_owner.setdefault(sem, eng)
        assert own == eng, f"dma sem {sem} used from {eng} and {own}"
        self.op(eng, fn, wait=wait, inc=((sem, 16),))
        return self.count[sem]

    def step(self, eng, fn, wait=()):
        self.op(eng, fn, wait=wait, inc=((f"e_{eng}", 1),))
        return self.count[f"e_{eng}"]

    def v(self, sem):
        return self.count.get(sem, 0)

    def flush(self):
        target = self.count["bar"] + len(ENGINES)
        for e in ENGINES:
            w = [(self.sems[sname], self.count[sname])
                 for sname, own in self.dma_owner.items()
                 if own == e and self.count[sname] > 0]
            self.items[e].append((w, None, [(self.sems["bar"], 1)]))
            self.items[e].append(([(self.sems["bar"], target)], None, []))
        self.count["bar"] = target
        items = self.items
        self.items = {e: [] for e in ENGINES}

        def runner(engine, seq):
            last = {}
            for waits, fn, incs in seq:
                for h, v in waits:
                    if last.get(h.num, -1) < v:
                        engine.wait_ge(h, v)
                        last[h.num] = v
                inst = fn(engine) if fn is not None else None
                for h, n in incs:
                    if inst is not None:
                        inst = inst.then_inc(h, n)
                        inst = None if inst is None else inst
                    else:
                        engine.sem_inc(h, n)

        with self.nc.Block() as blk:
            for e in ENGINES:
                getattr(blk, e)(lambda eng, seq=items[e]: runner(eng, seq))


def _selc():
    import ml_dtypes
    out = np.zeros((16, NS * 128), dtype=np.float32)
    for dc in range(NS):
        for m in range(128):
            out[2 * dc + m // 64, dc * 128 + m] = 1.0
    return np.ascontiguousarray(out.astype(ml_dtypes.bfloat16))


def _rope_rt():
    inv = 1.0 / 10000.0 ** (np.arange(0, 64, 2, dtype=np.float64) / 64.0)
    out = np.zeros((NS, 128, 128), dtype=np.float32)
    for s in range(NS):
        R = np.eye(128, dtype=np.float64)
        for hh in (2 * s, 2 * s + 1):
            base = (hh - 2 * s) * 64
            for j in range(8):
                th = hh * inv[j]
                c, sn = np.cos(th), np.sin(th)
                R[base + 2 * j, base + 2 * j] = c
                R[base + 2 * j, base + 2 * j + 1] = -sn
                R[base + 2 * j + 1, base + 2 * j] = sn
                R[base + 2 * j + 1, base + 2 * j + 1] = c
        out[s] = R.T.astype(np.float32)
    return np.ascontiguousarray(out.reshape(NS * 128, 128))


def build_nc():
    nc = bass.Bass("TRN2", target_bir_lowering=False, debug=False)

    xin = nc.dram_tensor("xt", [D, T], F32, kind="ExternalInput").ap()
    wq_n = nc.dram_tensor("wq_n", [D, D], F32, kind="ExternalInput").ap()
    wkb_n = nc.dram_tensor("wkb_n", [D, LAT], F32, kind="ExternalInput").ap()
    wka_t = nc.dram_tensor("wka_t", [D, LAT], F32, kind="ExternalInput").ap()
    wva_t = nc.dram_tensor("wva_t", [D, LAT], F32, kind="ExternalInput").ap()
    wvb_t = nc.dram_tensor("wvb_t", [LAT, D], F32, kind="ExternalInput").ap()
    wo_t = nc.dram_tensor("wo_t", [D, D], F32, kind="ExternalInput").ap()
    wgu_h = nc.dram_tensor("wgu_h", [NJ, 128, 2 * D], I8, kind="ExternalInput").ap()
    wd_h = nc.dram_tensor("wd_h", [NJ, 128, D], I8, kind="ExternalInput").ap()
    rt_in = nc.dram_tensor("rt", [NS * 128, 128], F32, kind="ExternalInput").ap()
    selc_in = nc.dram_tensor("selc", [16, NS * 128], BF16, kind="ExternalInput").ap()
    out = nc.dram_tensor("out", [D, T], F32, kind="ExternalOutput").ap()

    q_hbm = nc.dram_tensor("q_hbm", [D, T], BF16, kind="Internal").ap()
    o_hbm = nc.dram_tensor("o_hbm", [D, T], BF16, kind="Internal").ap()
    x2_hbm = nc.dram_tensor("x2_hbm", [D, T], F32, kind="Internal").ap()
    kcin = nc.dram_tensor("kcin", [LAT, T], BF16, kind="Internal").ap()
    kcout = nc.dram_tensor("kcout", [2 * LAT, T], BF16, kind="Internal").ap()
    vcin = nc.dram_tensor("vcin", [LAT, T], BF16, kind="Internal").ap()
    vcout = nc.dram_tensor("vcout", [2 * LAT, T], BF16, kind="Internal").ap()
    k2_hbm = nc.dram_tensor("k2_hbm", [D, SEQ], BF16, kind="Internal").ap()
    oz_hbm = nc.dram_tensor("oz_hbm", [16, T], BF16, kind="Internal").ap()
    RG = [[0, 1], [2, 3], [4, 5], [6, 7]]

    _uid = [0]

    def un(tag):
        _uid[0] += 1
        return f"{tag}_{_uid[0]}"

    ctx = contextlib.ExitStack()
    sems = {}
    for name in ("e_tensor", "e_scalar", "e_vector",
                 "e_gpsimd", "e_sync", "cc", "bar"):
        sems[name] = ctx.enter_context(nc.semaphore(name))
    P = Planner(nc, sems, ctx)

    onescol = ctx.enter_context(nc.sbuf_tensor("onescol", [128, 1], BF16))
    onesrow = ctx.enter_context(nc.sbuf_tensor("onesrow", [128, 128], BF16))
    ones32 = ctx.enter_context(nc.sbuf_tensor("ones32", [128, 128], F32))
    xb = ctx.enter_context(nc.sbuf_tensor("xb", [128, 1024], BF16))
    xsq = ctx.enter_context(nc.sbuf_tensor("xsq", [128, 1024], BF16))
    rows = ctx.enter_context(nc.sbuf_tensor("rows", [1, 4 * 1024], F32))
    silu_sb = ctx.enter_context(nc.sbuf_tensor("silu_sb", [128, 4 * 512], F32))
    wgup = ctx.enter_context(nc.sbuf_tensor("wgup", [128, 2 * 2048], BF16))
    wdp = ctx.enter_context(nc.sbuf_tensor("wdp", [128, 2 * 1024], BF16))
    stg = ctx.enter_context(nc.sbuf_tensor("stg", [128, 4 * 512], BF16))
    fsb = ctx.enter_context(nc.sbuf_tensor("fsb", [128, 2 * 512], F32))
    WKBR = ctx.enter_context(nc.sbuf_tensor("WKBR", [128, 2 * 1024], BF16))
    WVB = ctx.enter_context(nc.sbuf_tensor("WVB", [128, 2 * 1024], BF16))
    lnf = ctx.enter_context(nc.sbuf_tensor("lnf", [128, 1024], F32))

    def row(i):
        return rows[:, i * 1024:(i + 1) * 1024]

    P.op("gpsimd", lambda g: g.memset(onescol[:], 1.0))
    P.op("gpsimd", lambda g: g.memset(onesrow[:], 1.0))
    ebias = ctx.enter_context(nc.sbuf_tensor("ebias", [1, 2], F32))
    P.op("gpsimd", lambda g: g.memset(ones32[:], 1.0))
    P.op("gpsimd", lambda g: g.memset(ebias[0:1, 0:1], float(EPS * 127.0 * 127.0)))
    P.op("gpsimd", lambda g: g.memset(ebias[0:1, 1:2], float(EPS)),
         inc=(("e_gpsimd", 1),))
    INIT = P.v("e_gpsimd")

    def ln(src, s_stride, s_col0, dst, d_stride, d_col0, ncols, a_scale, ps,
           dst_wait=None, src_ready=None, ret_applies=False):
        assert ncols % 1024 == 0
        state = {"xb_free": [0, 0], "ba_free": 0}
        lastv = P.v("e_vector")

        def vstep(fn, extra=()):
            state["lv"] = P.step("vector", fn,
                                 wait=[("e_vector", state["lv"])] + list(extra))
            return state["lv"]
        state["lv"] = lastv
        applies = []
        for c0 in range(0, ncols, 1024):
            sw0 = ("e_vector", P.v("e_vector"))
            for s in range(NS):
                sc = src[:, s * s_stride + s_col0 + c0: s * s_stride + s_col0 + c0 + 1024]
                w = [("e_tensor", state["xb_free"][0])]
                if src_ready is not None:
                    w += list(src_ready(c0 // 1024, s))
                P.step("scalar", lambda sc_, sc=sc:
                       sc_.activation(out=xb[:], in_=sc, func=AF.Copy), wait=w)
                sqv = P.step("scalar", lambda sc_, sc=sc:
                             sc_.activation(out=xsq[:], in_=sc, func=AF.Square))
                for hh in range(2):
                    P.step("tensor", lambda t, s=s, hh=hh:
                           t.matmul(ps["psx"][:, hh * 512:(hh + 1) * 512], onescol[:],
                                    xb[:, hh * 512:(hh + 1) * 512],
                                    start=(s == 0), stop=(s == NS - 1),
                                    skip_group_check=True),
                           wait=([("e_scalar", sqv), ("e_gpsimd", INIT), sw0]
                                 if hh == 0 else []))
                    P.step("tensor", lambda t, s=s, hh=hh:
                           t.matmul(ps["psx2"][:, hh * 512:(hh + 1) * 512], onescol[:],
                                    xsq[:, hh * 512:(hh + 1) * 512],
                                    start=(s == 0), stop=(s == NS - 1),
                                    skip_group_check=True))
                state["xb_free"][0] = P.v("e_tensor")
            tstats = P.v("e_tensor")
            vstep(lambda v: v.tensor_scalar_mul(row(0), ps["psx"][:], 1.0 / D),
                  extra=[("e_tensor", tstats)])
            vstep(lambda v: v.tensor_scalar_mul(row(2), ps["psx2"][:], 1.0 / D))
            vstep(lambda v: v.tensor_mul(row(1), row(0), row(0)))
            vv = vstep(lambda v: v.tensor_sub(row(2), row(2), row(1)))
            inv2 = 1.0 / (a_scale * a_scale)
            bsl = 0 if a_scale != 1.0 else 1
            sv = P.step("scalar", lambda sc_, bsl=bsl: sc_.activation(
                out=row(2), in_=row(2), func=AF.Sqrt,
                scale=float(inv2), bias=ebias[0:1, bsl:bsl + 1]),
                wait=[("e_vector", vv), ("e_gpsimd", INIT)])
            vstep(lambda v: v.reciprocal(row(3), row(2)), extra=[("e_scalar", sv)])
            vv = vstep(lambda v: v.scalar_tensor_tensor(row(1), row(0), -1.0, row(3),
                                                        OP.mult, OP.mult))
            for hh in range(2):
                P.step("tensor", lambda t, hh=hh:
                       t.matmul(ps["pba"][:, hh * 512:(hh + 1) * 512], ones32[0:1, :],
                                row(3)[:, hh * 512:(hh + 1) * 512],
                                start=True, stop=True, skip_group_check=True),
                       wait=([("e_vector", vv), ("e_vector", state["ba_free"]),
                              ("e_gpsimd", INIT)] if hh == 0 else []))
            for hh in range(2):
                P.step("tensor", lambda t, hh=hh:
                       t.matmul(ps["pbb"][:, hh * 512:(hh + 1) * 512], ones32[0:1, :],
                                row(1)[:, hh * 512:(hh + 1) * 512],
                                start=True, stop=True, skip_group_check=True))
            tb = P.v("e_tensor")
            applies = []
            for s in range(NS):
                sc = src[:, s * s_stride + s_col0 + c0: s * s_stride + s_col0 + c0 + 1024]
                dc = dst[:, s * d_stride + d_col0 + c0: s * d_stride + d_col0 + c0 + 1024]
                fpar = s % 2
                extra = [("e_tensor", tb)]
                if s == 0 and dst_wait is not None:
                    extra.append(dst_wait)
                vstep(lambda v, sc=sc:
                      v.tensor_mul(lnf[:], sc, ps["pba"][:]),
                      extra=extra)
                lastv = vstep(lambda v, dc=dc:
                              v.tensor_add(dc, lnf[:], ps["pbb"][:]))
                applies.append(lastv)
            state["ba_free"] = lastv
        if ret_applies:
            return lastv, applies
        return lastv

    def ffn_gateup(st, xlnbuf, xl_stride, ncols, hbuf, pg, pu, wait_ready,
                   xl_col0=0):
        nu = ncols // 512
        for j in range(NJ):
            par = j % 2
            lu = P.dma("gpsimd", lambda g, j=j, par=par:
                       g.dma_start(out=wgup[:, par * 2048:(par + 1) * 2048],
                                   in_=wgu_h[j]),
                       f"qw{par}", wait=[st["wg_free"][par]])
            wr = wait_ready if isinstance(wait_ready, list) else [wait_ready] * NS
            for u in range(nu):
                for dc in range(NS):
                    w = [(f"qw{par}", lu), ("e_vector", wr[dc])]
                    if dc == 0 and u == 0:
                        w.append(st["pg_free"][par])
                    P.step("tensor", lambda t, par=par, dc=dc, u=u:
                           t.matmul(pg[par][:, u * 512:(u + 1) * 512],
                                    wgup[:, par * 2048 + dc * 128: par * 2048 + (dc + 1) * 128],
                                    xlnbuf[:, dc * xl_stride + xl_col0 + u * 512:
                                            dc * xl_stride + xl_col0 + (u + 1) * 512],
                                    start=(dc == 0), stop=(dc == NS - 1),
                                    skip_group_check=True), wait=w)
                tg = P.v("e_tensor")
                for dc in range(NS):
                    w = []
                    if dc == 0 and u == 0:
                        w.append(st["pu_free"][par])
                    P.step("tensor", lambda t, par=par, dc=dc, u=u:
                           t.matmul(pu[par][:, u * 512:(u + 1) * 512],
                                    wgup[:, par * 2048 + 1024 + dc * 128: par * 2048 + 1024 + (dc + 1) * 128],
                                    xlnbuf[:, dc * xl_stride + xl_col0 + u * 512:
                                            dc * xl_stride + xl_col0 + (u + 1) * 512],
                                    start=(dc == 0), stop=(dc == NS - 1),
                                    skip_group_check=True), wait=w)
                tu = P.v("e_tensor")
                if u == nu - 1:
                    st["wg_free"][par] = ("e_tensor", tu)
                slot = (j % 2) * 2 + u
                ss = P.step("scalar", lambda s, par=par, u=u, slot=slot:
                            s.activation(out=silu_sb[:, slot * 512:(slot + 1) * 512],
                                         in_=pg[par][:, u * 512:(u + 1) * 512],
                                         func=AF.Silu),
                            wait=[("e_tensor", tg), st["silu_free"][slot]])
                if u == nu - 1:
                    st["pg_free"][par] = ("e_scalar", ss)
                mv = P.step("vector", lambda v, j=j, par=par, u=u, slot=slot:
                            v.tensor_mul(hbuf[:, j * ncols + u * 512: j * ncols + (u + 1) * 512],
                                         silu_sb[:, slot * 512:(slot + 1) * 512],
                                         pu[par][:, u * 512:(u + 1) * 512]),
                            wait=[("e_scalar", ss), ("e_tensor", tu)])
                st["silu_free"][slot] = ("e_vector", mv)
                if u == nu - 1:
                    st["pu_free"][par] = ("e_vector", mv)
        return P.v("e_vector")

    def ffn_down(st, hbuf, ncols, pf, resid, r_stride, r_col0, hready):
        wd_src = wd_h
        nu = ncols // 512
        ng = NS // len(pf)
        lastv = 0
        addv_d = [0] * NS
        for u in range(nu):
            for g_ in range(ng):
                for j in range(NJ):
                    par = j % 2
                    ld = P.dma("gpsimd", lambda g, j=j, par=par, wd_src=wd_src:
                               g.dma_start(out=wdp[:, par * 1024:(par + 1) * 1024],
                                           in_=wd_src[j]),
                               f"qd{par}", wait=[st["wd_free"][par]])
                    for k_ in range(len(pf)):
                        i = g_ * len(pf) + k_
                        w = [(f"qd{par}", ld)]
                        if j == 0:
                            w += [("e_vector", hready), st["pf_free"]]
                        P.step("tensor", lambda t, par=par, i=i, k_=k_, j=j, u=u:
                               t.matmul(pf[k_][:],
                                        wdp[:, par * 1024 + i * 128: par * 1024 + (i + 1) * 128],
                                        hbuf[:, j * ncols + u * 512: j * ncols + (u + 1) * 512],
                                        start=(j == 0), stop=(j == NJ - 1),
                                        skip_group_check=True), wait=w)
                    st["wd_free"][par] = ("e_tensor", P.v("e_tensor"))
                tdown = P.v("e_tensor")
                for k_ in range(len(pf)):
                    i = g_ * len(pf) + k_
                    par = i % 2
                    ss = P.step("scalar", lambda s, k_=k_, par=par:
                                s.activation(out=fsb[:, par * 512:(par + 1) * 512],
                                             in_=pf[k_][:], func=AF.Copy, scale=0.5 / 127.0),
                                wait=[("e_tensor", tdown), st["fsb_free"][par]])
                    rsl = resid[:, i * r_stride + r_col0 + u * 512:
                                i * r_stride + r_col0 + (u + 1) * 512]
                    lastv = P.step("vector", lambda v, rsl=rsl, par=par:
                                   v.tensor_add(rsl, rsl, fsb[:, par * 512:(par + 1) * 512]),
                                   wait=[("e_scalar", ss)])
                    addv_d[i] = lastv
                    st["fsb_free"][par] = ("e_vector", lastv)
                st["pf_free"] = ("e_vector", lastv)
        return lastv, addv_d

    def new_ffn_state():
        z = ("e_tensor", 0)
        zv = ("e_vector", 0)
        zs = ("e_scalar", 0)
        return dict(wg_free=[z, z], pg_free=[zs, zs], pu_free=[zv, zv],
                    wd_free=[z, z], pf_free=zv, fsb_free=[zv, zv],
                    silu_free=[zv, zv, zv, zv], wb_free=[0, 0], wbd_free=[0, 0])

    PF_NAMES = [f"pf{i}" for i in range(8)]

    def pf_scope(es, tag):
        return [es.enter_context(nc.psum_tensor(un(f"{tag}{i}"), [128, 512], F32))
                for i in range(8)]

    with nc.sbuf_tensor("X", [128, NS * T], F32) as X, \
         nc.sbuf_tensor("XLN", [128, NS * T], BF16) as XLN:

        xld = 0
        for s in range(NS):
            xld = P.dma("sync", lambda e, s=s:
                        e.dma_start(out=X[:, s * T:(s + 1) * T],
                                    in_=xin[s * 128:(s + 1) * 128, :]), "qx")

        def dump_X():
            vv = P.v("e_vector")
            last = 0
            for s_ in range(NS):
                last = P.dma("sync", lambda e, s_=s_:
                             e.dma_start(out=out[s_ * 128:(s_ + 1) * 128, :],
                                         in_=X[:, s_ * T:(s_ + 1) * T]), "qx",
                             wait=[("e_vector", vv), ("qx", xld)])
            P.op("sync", None, wait=[("qx", last)], inc=(("e_sync", 1),))
            P.flush()

        if FINE == "load":
            dump_X()
            return nc

        with nc.sbuf_tensor("H", [128, NJ * TCF], BF16) as H, \
             nc.psum_tensor("PB0", [128, 1024], F32) as PB0, \
             nc.psum_tensor("PB1", [128, 1024], F32) as PB1, \
             nc.psum_tensor("PB2", [128, 1024], F32) as PB2, \
             nc.psum_tensor("PB3", [128, 1024], F32) as PB3:
            st1 = new_ffn_state()

            def ln1(tc, first=False):
                if first:
                    P.op("vector", None, wait=[("qx", xld)], inc=(("e_vector", 1),))
                ps = dict(psx=PB2[0:1, :], psx2=PB3[0:1, :], pba=PB2, pbb=PB3)
                return ln(X, T, tc * TCF, XLN, T, tc * TCF, TCF, 1.0 / 127.0, ps,
                          dst_wait=("e_tensor", P.v("e_tensor")),
                          src_ready=(lambda ci, s: [("qx", xld)]) if first else None,
                          ret_applies=True)

            pf8 = [PB0[:, 0:512], PB0[:, 512:1024], PB1[:, 0:512], PB1[:, 512:1024],
                   PB2[:, 0:512], PB2[:, 512:1024], PB3[:, 0:512], PB3[:, 512:1024]]
            xl1ready, ap1 = ln1(0, first=True)
            for tc in range(NCF):
                hready = ffn_gateup(st1, XLN, T, TCF, H, [PB0, PB1],
                                    [PB2, PB3], ap1,
                                    xl_col0=tc * TCF)
                if tc + 1 < NCF:
                    xl1ready, ap1 = ln1(tc + 1)
                    st1["pf_free"] = ("e_vector", xl1ready)
                else:
                    st1["pf_free"] = st1["pg_free"][1]
                vdone, ad1 = ffn_down(st1, H, TCF, pf8, X, T, tc * TCF, hready)
                st1["pg_free"] = [("e_vector", ad1[1]), ("e_vector", ad1[3])]
                st1["pu_free"] = [("e_vector", ad1[5]), ("e_vector", ad1[7])]
            P.flush()

        with nc.sbuf_tensor("WQR", [128, NS * 1024], BF16) as WQR:
            with nc.sbuf_tensor("WQN", [128, NS * 1024], BF16) as WQN, \
                 nc.sbuf_tensor("WKBN", [128, NS * 256], BF16) as WKBN, \
                 nc.sbuf_tensor("RT", [128, NS * 128], BF16) as RT, \
                 nc.psum_tensor("prf0", [128, 128], F32) as prf0, \
                 nc.psum_tensor("prf1", [128, 128], F32) as prf1:
                prf = [prf0, prf1]
                strip_ld = [None] * NS
                pv3 = [0, 0, 0]
                for s in range(NS):
                    pv3[0] = P.dma("gpsimd", lambda g, s=s:
                                   g.dma_start(out=RT[:, s * 128:(s + 1) * 128],
                                               in_=rt_in[s * 128:(s + 1) * 128, :]),
                                   "qrf0",
                                   wait=([("qrf0", pv3[0])] if s else []))
                    pv3[1] = P.dma("gpsimd", lambda g, s=s:
                                   g.dma_start(out=WQN[:, s * 1024:(s + 1) * 1024],
                                               in_=wq_n[s * 128:(s + 1) * 128, :]),
                                   "qrf1",
                                   wait=([("qrf1", pv3[1])] if s else []))
                    pv3[2] = P.dma("gpsimd", lambda g, s=s:
                                   g.dma_start(out=WKBN[:, s * 256:(s + 1) * 256],
                                               in_=wkb_n[s * 128:(s + 1) * 128, :]),
                                   "qrf2",
                                   wait=([("qrf2", pv3[2])] if s else []))
                    strip_ld[s] = [("qrf0", pv3[0]), ("qrf1", pv3[1]),
                                   ("qrf2", pv3[2])]
                evfree = [0, 0]
                k = 0
                for fs in range(NS):
                    for dc in range(NS):
                        par = k % 2
                        w = [("e_vector", evfree[par])]
                        if dc == 0:
                            w += strip_ld[fs]
                        tmm = P.step("tensor", lambda t, dc=dc, fs=fs, par=par:
                                     t.matmul(prf[par][:],
                                              WQN[:, fs * 1024 + dc * 128: fs * 1024 + (dc + 1) * 128],
                                              RT[:, fs * 128:(fs + 1) * 128],
                                              start=True, stop=True,
                                              skip_group_check=True),
                                     wait=w)
                        evfree[par] = P.step(
                            "vector", lambda v, dc=dc, fs=fs, par=par:
                            v.tensor_copy(WQR[:, dc * 1024 + fs * 128: dc * 1024 + (fs + 1) * 128],
                                          prf[par][:]),
                            wait=[("e_tensor", tmm)])
                        k += 1
                    for lc in range(2):
                        par = k % 2
                        tmm = P.step("tensor", lambda t, lc=lc, fs=fs, par=par:
                                     t.matmul(prf[par][:],
                                              WKBN[:, fs * 256 + lc * 128: fs * 256 + (lc + 1) * 128],
                                              RT[:, fs * 128:(fs + 1) * 128],
                                              start=True, stop=True,
                                              skip_group_check=True),
                                     wait=[("e_vector", evfree[par])])
                        evfree[par] = P.step(
                            "vector", lambda v, lc=lc, fs=fs, par=par:
                            v.tensor_copy(WKBR[:, lc * 1024 + fs * 128: lc * 1024 + (fs + 1) * 128],
                                          prf[par][:]),
                            wait=[("e_tensor", tmm)])
                        k += 1
                P.flush()

            with nc.sbuf_tensor("WKA", [128, NS * 256], BF16) as WKA, \
                 nc.sbuf_tensor("WVA", [128, NS * 256], BF16) as WVA, \
                 nc.sbuf_tensor("QSTG", [128, 8 * 512], BF16) as QSTG, \
                 nc.psum_tensor("pp0", [128, 512], F32) as pp0, \
                 nc.psum_tensor("pp1", [128, 512], F32) as pp1, \
                 nc.psum_tensor("pp2", [128, 512], F32) as pp2, \
                 nc.psum_tensor("pp3", [128, 512], F32) as pp3, \
                 nc.psum_tensor("pbaB", [128, 1024], F32) as pba, \
                 nc.psum_tensor("pbbB", [128, 1024], F32) as pbb:
                pp = [pp0, pp1, pp2, pp3]
                wl = 0
                for s in range(NS):
                    P.dma("gpsimd", lambda g, s=s:
                          g.dma_start(out=WKA[:, s * 256:(s + 1) * 256],
                                      in_=wka_t[s * 128:(s + 1) * 128, :]), "qaw")
                    wl = P.dma("gpsimd", lambda g, s=s:
                               g.dma_start(out=WVA[:, s * 256:(s + 1) * 256],
                                           in_=wva_t[s * 128:(s + 1) * 128, :]), "qaw")
                for lc in range(2):
                    wl = P.dma("gpsimd", lambda g, lc=lc:
                               g.dma_start(out=WVB[:, lc * 1024:(lc + 1) * 1024],
                                           in_=wvb_t[lc * 128:(lc + 1) * 128, :]), "qaw")

                kslot = [("e_scalar", 0)] * 4
                qstg_last = [(f"qqw{i}", 0) for i in range(8)]
                stg_last = [("qqw0", 0), ("qqw1", 0), ("qqw2", 0), ("qqw3", 0)]
                kidx = [0]

                def proj(lhs_fn, n_out_blocks, rhs_fn, nacc, evac_fn, wait0=()):
                    for ob in range(n_out_blocks):
                        sl = kidx[0] % 4
                        kidx[0] += 1
                        for a in range(nacc):
                            w = list(wait0)
                            if a == 0:
                                w.append(kslot[sl])
                            P.step("tensor", lambda t, ob=ob, a=a, sl=sl:
                                   t.matmul(pp[sl][:], lhs_fn(ob, a), rhs_fn(ob, a),
                                            start=(a == 0), stop=(a == nacc - 1),
                                            skip_group_check=True), wait=w)
                        kslot[sl] = evac_fn(ob, pp[sl], P.v("e_tensor"))

                def hbm_evac(dst, sem_pfx):
                    def evac(ob_dst_cols, psum, tmm, par):
                        ev = P.step("scalar", lambda s_, psum=psum, par=par:
                                    s_.activation(out=stg[:, par * 512:(par + 1) * 512],
                                                  in_=psum[:], func=AF.Copy),
                                    wait=[("e_tensor", tmm), stg_last[par]])
                        stg_last[par] = (f"{sem_pfx}{par}", P.dma(
                            "sync", lambda e, ob_dst_cols=ob_dst_cols, par=par:
                            e.dma_start(out=ob_dst_cols,
                                        in_=stg[:, par * 512:(par + 1) * 512]),
                            f"{sem_pfx}{par}", wait=[("e_scalar", ev)]))
                        return ("e_scalar", ev)
                    return evac

                xl2 = 0
                for tc in range(NCF):
                    ps = dict(psx=pba[0:1, :], psx2=pbb[0:1, :], pba=pba, pbb=pbb)
                    xl2 = ln(X, T, tc * TCF, XLN, T, tc * TCF, TCF, 1.0, ps,
                             dst_wait=None)
                    for s in range(NS):
                        P.dma("sync", lambda e, s=s, tc=tc:
                              e.dma_start(out=x2_hbm[s * 128:(s + 1) * 128,
                                                     tc * TCF:(tc + 1) * TCF],
                                          in_=X[:, s * T + tc * TCF: s * T + (tc + 1) * TCF]),
                              "qx2", wait=[("e_vector", xl2)])

                kev = hbm_evac(kcin, "qkc")

                def klevac(ob, psum, tmm):
                    ls, u = divmod(ob, 4)
                    return kev(kcin[ls * 128:(ls + 1) * 128, u * 512:(u + 1) * 512],
                               psum, tmm, ob % 4)

                proj(lambda ob, a: WKA[:, a * 256 + (ob // 4) * 128: a * 256 + (ob // 4) * 128 + 128],
                     2 * 4,
                     lambda ob, a: XLN[:, a * T + (ob % 4) * 512: a * T + (ob % 4) * 512 + 512],
                     NS, klevac, wait0=[("e_vector", xl2), ("qaw", wl)])
                P.op("gpsimd", lambda g:
                     g.collective_compute("AllGather", OP.bypass, replica_groups=RG,
                                          ins=[kcin], outs=[kcout]),
                     wait=[(f"qkc{i}", P.v(f"qkc{i}")) for i in range(4)],
                     inc=(("cc", 1),))

                vev = hbm_evac(vcin, "qvc")

                def vlevac(ob, psum, tmm):
                    ls, u = divmod(ob, 4)
                    return vev(vcin[ls * 128:(ls + 1) * 128, u * 512:(u + 1) * 512],
                               psum, tmm, ob % 4)

                proj(lambda ob, a: WVA[:, a * 256 + (ob // 4) * 128: a * 256 + (ob // 4) * 128 + 128],
                     2 * 4,
                     lambda ob, a: XLN[:, a * T + (ob % 4) * 512: a * T + (ob % 4) * 512 + 512],
                     NS, vlevac)
                P.op("gpsimd", lambda g:
                     g.collective_compute("AllGather", OP.bypass, replica_groups=RG,
                                          ins=[vcin], outs=[vcout]),
                     wait=[(f"qvc{i}", P.v(f"qvc{i}")) for i in range(4)],
                     inc=(("cc", 1),))

                def qevac(ob, psum, tmm):
                    fs, u = divmod(ob, 4)
                    par = ob % 8
                    ev = P.step("scalar", lambda s_, psum=psum, par=par:
                                s_.activation(out=QSTG[:, par * 512:(par + 1) * 512],
                                              in_=psum[:], func=AF.Copy),
                                wait=[("e_tensor", tmm), qstg_last[par]])
                    qstg_last[par] = (f"qqw{par}", P.dma(
                        "sync", lambda e, fs=fs, u=u, par=par:
                        e.dma_start(out=q_hbm[fs * 128:(fs + 1) * 128,
                                              u * 512:(u + 1) * 512],
                                    in_=QSTG[:, par * 512:(par + 1) * 512]),
                        f"qqw{par}", wait=[("e_scalar", ev)]))
                    return ("e_scalar", ev)

                proj(lambda ob, a: WQR[:, a * 1024 + (ob // 4) * 128: a * 1024 + (ob // 4) * 128 + 128],
                     NS * 4,
                     lambda ob, a: XLN[:, a * T + (ob % 4) * 512: a * T + (ob % 4) * 512 + 512],
                     NS, qevac)
                P.flush()

    with nc.sbuf_tensor("WOT", [128, NS * 1024], BF16) as WOT:
        wotld = 0
        for s in range(NS):
            wotld = P.dma("gpsimd", lambda g, s=s:
                          g.dma_start(out=WOT[:, s * 1024:(s + 1) * 1024],
                                      in_=wo_t[s * 128:(s + 1) * 128, :]), "qwo")

        with nc.sbuf_tensor("VSB", [128, NKT * 1056], BF16) as VSB, \
             nc.sbuf_tensor("KB", [128, 2 * SEQ], BF16) as KB, \
             nc.sbuf_tensor("QB", [128, 2 * QCW], BF16) as QB, \
             nc.sbuf_tensor("PT", [128, 4 * 1024], BF16) as PT, \
             nc.sbuf_tensor("ONST", [128, 4 * 512], BF16) as ONST, \
             nc.sbuf_tensor("KLT", [128, 4 * 512], BF16) as KLT, \
             nc.sbuf_tensor("VLT", [128, 2 * 2048], BF16) as VLT, \
             nc.psum_tensor("S00", [128, 1024], F32) as S00, \
             nc.psum_tensor("S10", [128, 1024], F32) as S10, \
             nc.psum_tensor("OT00", [128, 512], F32) as OT00, \
             nc.psum_tensor("OT01", [128, 512], F32) as OT01, \
             nc.psum_tensor("OT10", [128, 512], F32) as OT10, \
             nc.psum_tensor("OT11", [128, 512], F32) as OT11:
            SS = [S00, S10]
            OT = [[OT00, OT01], [OT10, OT11]]

            genp = [S00[:, 0:512], S00[:, 512:1024], S10[:, 0:512], S10[:, 512:1024]]
            gfree = [0, 0, 0, 0]
            stg_last2 = [("qk2w0", 0), ("qk2w1", 0), ("qk2w2", 0), ("qk2w3", 0)]
            gi = 0
            kl_use = [0, 0]
            kl_ld = [0, 0]
            for tch in range(NS):
                par = tch % 2
                slot = tch // 4
                for lc in range(2):
                    kl_ld[par] = P.dma(
                        "sync", lambda e, slot=slot, lc=lc, tch=tch, par=par:
                        e.dma_start(out=KLT[:, (par * 2 + lc) * 512:(par * 2 + lc + 1) * 512],
                                    in_=kcout[slot * LAT + lc * 128: slot * LAT + (lc + 1) * 128,
                                              (tch % 4) * 512:(tch % 4) * 512 + 512]),
                        f"qlk{par}",
                        wait=([("cc", 1), ("e_tensor", kl_use[par])] if lc == 0 else []))
                for fs in range(NS):
                    sl = gi % 4
                    gi += 1
                    for lc in range(2):
                        w = [(f"qlk{par}", kl_ld[par])]
                        if lc == 0:
                            w.append(("e_vector", gfree[sl]))
                        P.step("tensor", lambda t, fs=fs, lc=lc, sl=sl, par=par:
                               t.matmul(genp[sl][:],
                                        WKBR[:, lc * 1024 + fs * 128: lc * 1024 + (fs + 1) * 128],
                                        KLT[:, (par * 2 + lc) * 512:(par * 2 + lc + 1) * 512],
                                        start=(lc == 0), stop=(lc == 1),
                                        skip_group_check=True), wait=w)
                    kl_use[par] = P.v("e_tensor")
                    tmm = kl_use[par]
                    spar = sl
                    ev = P.step("vector", lambda v, spar=spar, sl=sl:
                                v.tensor_copy(stg[:, spar * 512:(spar + 1) * 512], genp[sl][:]),
                                wait=[("e_tensor", tmm), stg_last2[spar]])
                    gfree[sl] = ev
                    stg_last2[spar] = (f"qk2w{spar}", P.dma(
                        "sync", lambda e, fs=fs, tch=tch, spar=spar:
                        e.dma_start(out=k2_hbm[fs * 128:(fs + 1) * 128,
                                               tch * 512:(tch + 1) * 512],
                                    in_=stg[:, spar * 512:(spar + 1) * 512]),
                        f"qk2w{spar}", wait=[("e_vector", ev)]))
            vmset = P.step("vector", lambda v: v.memset(VSB[:], 1.0))
            vl_use = 0
            vl_ld = 0
            for kt in range(NKT):
                slot = kt // 16
                if kt % 16 == 0:
                    for lc in range(2):
                        vl_ld = P.dma(
                            "sync", lambda e, slot=slot, lc=lc:
                            e.dma_start(out=VLT[:, lc * 2048:(lc + 1) * 2048],
                                        in_=vcout[slot * LAT + lc * 128:
                                                  slot * LAT + (lc + 1) * 128, :]),
                            "qlv0",
                            wait=([("cc", 2), ("e_tensor", vl_use)] if lc == 0 else []))
                for fc in range(2):
                    sl = gi % 4
                    gi += 1
                    for lc in range(2):
                        w = [("qlv0", vl_ld)]
                        if lc == 0:
                            w.append(("e_vector", gfree[sl]))
                        P.step("tensor", lambda t, kt=kt, fc=fc, lc=lc, sl=sl:
                               t.matmul(genp[sl][:],
                                        VLT[:, lc * 2048 + (kt % 16) * 128:
                                            lc * 2048 + (kt % 16) * 128 + 128],
                                        WVB[:, lc * 1024 + fc * 512: lc * 1024 + (fc + 1) * 512],
                                        start=(lc == 0), stop=(lc == 1),
                                        skip_group_check=True), wait=w)
                    vl_use = P.v("e_tensor")
                    tmm = vl_use
                    dst3 = VSB[:, kt * 1056 + fc * 8 * 66: kt * 1056 + (fc * 8 + 8) * 66] \
                        .rearrange("p (h f) -> p h f", f=66)[:, :, 0:64]
                    src3 = genp[sl].rearrange("p (h f) -> p h f", f=64)
                    gfree[sl] = P.step("vector", lambda v, dst3=dst3, src3=src3:
                                       v.tensor_copy(dst3, src3),
                                       wait=[("e_tensor", tmm), ("e_vector", vmset)])
            vgen_done = P.v("e_vector")

            kld = [0, 0]
            kuse = [0, 0]
            qld = [0, 0]
            quse = [0, 0]
            exp_done = [0, 0]
            pt_use = [0, 0]
            ot_evac = [0, 0]
            sB_free = 0
            onst_dma = [0, 0]

            k2done = [P.v(f"qk2w{i}") for i in range(4)]

            def load_k(p, par):
                w = [("e_tensor", kuse[par])]
                w += [(f"qk2w{i}", k2done[i]) for i in range(4)]
                kld[par] = P.dma("sync", lambda e, p=p, par=par:
                                 e.dma_start(out=KB[:, par * SEQ:(par + 1) * SEQ],
                                             in_=k2_hbm[p * 128:(p + 1) * 128, :]),
                                 f"qk{par}", wait=w)

            def load_q(p, qc, par):
                qld[par] = P.dma("sync", lambda e, p=p, qc=qc, par=par:
                                 e.dma_start(out=QB[:, par * QCW:(par + 1) * QCW],
                                             in_=q_hbm[p * 128:(p + 1) * 128,
                                                       qc * QCW:(qc + 1) * QCW]),
                                 f"qq{par}", wait=[("e_tensor", quse[par])])

            load_k(0, 0)
            load_q(0, 0, 0)
            it = 0
            NIT = NP * NQC

            def emit_pv(p, kt, opar):
                spar = kt % 2
                w = [("e_scalar", exp_done[spar]), ("e_vector", vgen_done)]
                if kt == 0:
                    w.append(("e_vector", ot_evac[opar]))
                first = [True]
                for h in range(2):
                    OTP = OT[h][opar]
                    base = kt * 1056 + (2 * p + h) * 66
                    P.step("tensor", lambda t, OTP=OTP, base=base, kt=kt,
                           h=h, spar=spar:
                           t.matmul(OTP[0:65, :],
                                    VSB[:, base: base + 65],
                                    PT[:, spar * 1024 + h * 512: spar * 1024 + (h + 1) * 512],
                                    start=(kt == 0), stop=(kt == NKT - 1),
                                    skip_group_check=True),
                           wait=(w if first[0] else ()))
                    first[0] = False
                pt_use[spar] = P.v("e_tensor")

            for p in range(NP):
                kpar = p % 2
                if p + 1 < NP:
                    load_k(p + 1, (p + 1) % 2)
                for qc in range(NQC):
                    qpar = it % 2
                    opar = it % 2
                    if it + 1 < NIT:
                        np_, nqc = divmod(it + 1, NQC)
                        load_q(np_, nqc, (it + 1) % 2)
                    for kt in range(NKT):
                        spar = kt % 2
                        for h in range(2):
                            rs = 0 if h == 0 else 64
                            w = []
                            if h == 0:
                                w = [(f"qk{kpar}", kld[kpar]),
                                     (f"qq{qpar}", qld[qpar]),
                                     ("e_scalar", exp_done[spar])]
                                if kt <= 1:
                                    w.append(("e_vector", vgen_done))
                            P.step("tensor", lambda t, rs=rs, kt=kt, h=h,
                                   spar=spar, kpar=kpar, qpar=qpar:
                                   t.matmul(SS[spar][:, h * 512:(h + 1) * 512],
                                            KB[rs:rs + 64,
                                               kpar * SEQ + kt * 128: kpar * SEQ + (kt + 1) * 128],
                                            QB[rs:rs + 64, qpar * QCW:(qpar + 1) * QCW],
                                            start=True, stop=True,
                                            tile_position=(rs, 0),
                                            skip_group_check=True), wait=w)
                        smm = P.v("e_tensor")
                        quse[qpar] = smm
                        kuse[kpar] = smm
                        exp_done[spar] = P.step(
                            "scalar", lambda sc, spar=spar:
                            sc.activation(out=PT[:, spar * 1024:(spar + 1) * 1024],
                                          in_=SS[spar][:], func=AF.Exp, scale=0.125),
                            wait=[("e_tensor", smm), ("e_tensor", pt_use[spar])])
                        if kt > 0:
                            emit_pv(p, kt - 1, opar)
                    emit_pv(p, NKT - 1, opar)
                    pvdone = P.v("e_tensor")
                    n0 = P.step("vector", lambda v, opar=opar:
                                v.tensor_copy(ONST[0:65, (2 * opar) * 512:(2 * opar + 1) * 512],
                                              OT[0][opar][0:65, :]),
                                wait=[("e_tensor", pvdone),
                                      (f"qo{opar}", onst_dma[opar])])
                    n1 = P.step("vector", lambda v, opar=opar:
                                v.tensor_copy(ONST[0:65, (2 * opar + 1) * 512:(2 * opar + 2) * 512],
                                              OT[1][opar][0:65, :]))
                    ot_evac[opar] = n1
                    P.dma("sync", lambda e, p=p, qc=qc, opar=opar:
                          e.dma_start(out=o_hbm[p * 128:p * 128 + 64,
                                                qc * QCW:(qc + 1) * QCW],
                                      in_=ONST[0:64, (2 * opar) * 512:(2 * opar + 1) * 512]),
                          f"qo{opar}", wait=[("e_vector", n1)])
                    P.dma("sync", lambda e, p=p, qc=qc, opar=opar:
                          e.dma_start(out=o_hbm[p * 128 + 64:(p + 1) * 128,
                                                qc * QCW:(qc + 1) * QCW],
                                      in_=ONST[0:64, (2 * opar + 1) * 512:(2 * opar + 2) * 512]),
                          f"qo{opar}")
                    P.dma("sync", lambda e, p=p, qc=qc, opar=opar:
                          e.dma_start(out=oz_hbm[2 * p:2 * p + 1, qc * QCW:(qc + 1) * QCW],
                                      in_=ONST[64:65, (2 * opar) * 512:(2 * opar + 1) * 512]),
                          f"qo{opar}")
                    onst_dma[opar] = P.dma(
                        "sync", lambda e, p=p, qc=qc, opar=opar:
                        e.dma_start(out=oz_hbm[2 * p + 1:2 * p + 2, qc * QCW:(qc + 1) * QCW],
                                    in_=ONST[64:65, (2 * opar + 1) * 512:(2 * opar + 2) * 512]),
                        f"qo{opar}", wait=[("e_vector", n1)])
                    it += 1
            P.flush()

        if TRUNC == 3:
            with nc.sbuf_tensor("odmp", [128, T], BF16) as odmp, \
                 nc.sbuf_tensor("odmp2", [128, T], F32) as odmp2:
                for s_ in range(NS):
                    l1 = P.dma("sync", lambda e, s_=s_:
                               e.dma_start(out=odmp[:], in_=o_hbm[s_ * 128:(s_ + 1) * 128, :]),
                               "qdump")
                    c1 = P.step("vector", lambda v: v.tensor_copy(odmp2[:], odmp[:]),
                                wait=[("qdump", l1)])
                    l2 = P.dma("sync", lambda e, s_=s_:
                               e.dma_start(out=out[s_ * 128:(s_ + 1) * 128, :], in_=odmp2[:]),
                               "qdump", wait=[("e_vector", c1)])
                P.op("sync", None, wait=[("qdump", l2)], inc=(("e_sync", 1),))
                P.flush()
            return nc

        with nc.sbuf_tensor("OBUF", [128, NS * 1024], BF16) as OBUF, \
             nc.sbuf_tensor("SELC", [16, NS * 128], BF16) as SELC, \
             nc.sbuf_tensor("ZBT", [128, 2 * 512], BF16) as ZBT, \
             nc.sbuf_tensor("X3C", [128, NS * 1024], F32) as X3C, \
             nc.sbuf_tensor("XLN3", [128, NS * 1024], BF16) as XLN3, \
             nc.sbuf_tensor("H2", [128, NJ * 1024], BF16) as H2, \
             nc.psum_tensor("PA0", [128, 1024], F32) as PA0, \
             nc.psum_tensor("PA1", [128, 1024], F32) as PA1, \
             nc.psum_tensor("PA2", [128, 1024], F32) as PA2, \
             nc.psum_tensor("PA3", [128, 1024], F32) as PA3:
            PA = [PA0, PA1, PA2, PA3]
            st2 = new_ffn_state()
            st2["pf_free"] = ("e_scalar", 0)
            wo_last = 0
            outst = 0
            fin = 0

            OZC = silu_sb[0:16, 0:2048]
            OZRv = stg[0:16, 0:2048]
            oz_ld = P.dma("gpsimd", lambda g:
                          g.dma_start(out=OZC, in_=oz_hbm), "qoz")
            ozv = P.step("vector", lambda v:
                         v.reciprocal(OZC, OZC),
                         wait=[("qoz", oz_ld)])
            ozv = P.step("vector", lambda v:
                         v.tensor_copy(OZRv, OZC),
                         wait=[("e_vector", ozv)])
            st2["silu_free"] = [("e_vector", ozv)] * 4
            selld = P.dma("gpsimd", lambda g:
                          g.dma_start(out=SELC[:], in_=selc_in), "qwo")
            nrm_prev = [0, 0]

            x2q = [0]
            st_out = [0] * NS

            def wo_ln3(c):
                nonlocal wo_last
                o_ld = 0
                for half in range(2):
                    cc = 2 * c + half
                    osrc = o_hbm[:, cc * 512:(cc + 1) * 512].rearrange(
                        "(s p) t -> p s t", p=128)
                    odst = OBUF[:, half * NS * 512:(half + 1) * NS * 512].rearrange(
                        "p (s t) -> p s t", t=512)
                    o_ld = P.dma("sync", lambda e, osrc=osrc, odst=odst:
                                 e.dma_start(out=odst, in_=osrc), "qc0",
                                 wait=[("e_tensor", wo_last)])
                x2l = [0] * NS
                for s_ in range(NS):
                    w = [("qxr0", x2q[0])]
                    if c == 1:
                        w += [("qout", st_out[s_]), ("e_scalar", P.v("e_scalar"))]
                    x2q[0] = P.dma("sync", lambda e, s_=s_, c=c:
                                   e.dma_start(out=X3C[:, s_ * 1024:(s_ + 1) * 1024],
                                               in_=x2_hbm[s_ * 128:(s_ + 1) * 128,
                                                          c * 1024:(c + 1) * 1024]),
                                   "qxr0", wait=w)
                    x2l[s_] = x2q[0]
                nrm = {}
                for half in range(2):
                    cc = 2 * c + half
                    hb = half * NS * 512
                    for dc in range(NS):
                        k2 = dc % 2
                        bmm = P.step("tensor", lambda t, dc=dc, cc=cc, k2=k2:
                                     t.matmul(PA[2 + k2][0:128, 0:512],
                                              SELC[:, dc * 128:(dc + 1) * 128],
                                              OZRv[:, cc * 512:(cc + 1) * 512],
                                              start=True, stop=True,
                                              skip_group_check=True),
                                     wait=[("e_vector", ozv), ("qwo", selld),
                                           ("e_vector", nrm_prev[k2]),
                                           st2["pu_free"][k2]])
                        zc = P.step("vector", lambda v, k2=k2:
                                    v.tensor_copy(ZBT[:, k2 * 512:(k2 + 1) * 512],
                                                  PA[2 + k2][0:128, 0:512]),
                                    wait=[("e_tensor", bmm)])
                        nrm[(half, dc)] = P.step(
                            "vector", lambda v, dc=dc, hb=hb, k2=k2:
                            v.tensor_mul(OBUF[:, hb + dc * 512: hb + (dc + 1) * 512],
                                         OBUF[:, hb + dc * 512: hb + (dc + 1) * 512],
                                         ZBT[:, k2 * 512:(k2 + 1) * 512]),
                            wait=[("e_vector", zc), ("qc0", o_ld)])
                        nrm_prev[k2] = nrm[(half, dc)]
                st2["pu_free"] = [("e_vector", nrm_prev[0]), ("e_vector", nrm_prev[1])]
                lastev = 0
                rot = [PA0[:, 0:512], PA0[:, 512:1024], PA1[:, 0:512], PA1[:, 512:1024]]
                rot_free = [st2["pg_free"][0], st2["pg_free"][0],
                            st2["pg_free"][1], st2["pg_free"][1]]
                addv = [0] * NS
                for half in range(2):
                    hb = half * NS * 512
                    for fs in range(NS):
                        k4 = (half * NS + fs) % 4
                        for dc in range(NS):
                            w = [("qc0", o_ld), ("qwo", wotld),
                                 ("e_vector", nrm[(half, dc)])]
                            if dc == 0:
                                w.append(rot_free[k4])
                            P.step("tensor", lambda t, fs=fs, dc=dc, k4=k4, hb=hb:
                                   t.matmul(rot[k4][:],
                                            WOT[:, dc * 1024 + fs * 128: dc * 1024 + (fs + 1) * 128],
                                            OBUF[:, hb + dc * 512: hb + (dc + 1) * 512],
                                            start=(dc == 0), stop=(dc == NS - 1),
                                            skip_group_check=True), wait=w)
                        tmm = P.v("e_tensor")
                        x3sl = X3C[:, fs * 1024 + half * 512: fs * 1024 + (half + 1) * 512]
                        lastev = P.step("vector", lambda v, x3sl=x3sl, k4=k4:
                                        v.tensor_add(x3sl, x3sl, rot[k4][:]),
                                        wait=[("e_tensor", tmm), ("qxr0", x2l[fs])])
                        rot_free[k4] = ("e_vector", lastev)
                        addv[fs] = lastev
                wo_last = P.v("e_tensor")
                ps = dict(psx=PA2[0:1, :], psx2=PA3[0:1, :], pba=PA2, pbb=PA3)
                xl3r, ap3 = ln(X3C, 1024, 0, XLN3, 1024, 0, 1024, 1.0 / 127.0, ps,
                               dst_wait=("e_tensor", P.v("e_tensor")),
                               src_ready=lambda ci, s: [("e_vector", addv[s])],
                               ret_applies=True)
                st2["pg_free"] = [("e_vector", lastev), ("e_vector", lastev)]
                st2["pu_free"] = [("e_vector", xl3r), ("e_vector", xl3r)]
                return xl3r, ap3

            for c in range(2):
                xl3ready, ap3 = wo_ln3(c)
                hready = ffn_gateup(st2, XLN3, 1024, 1024, H2, [PA0, PA1],
                                    [PA2, PA3], ap3)
                st2["pf_free"] = st2["pg_free"][1]
                pf8 = [PA0[:, 0:512], PA0[:, 512:1024], PA1[:, 0:512], PA1[:, 512:1024],
                       PA2[:, 0:512], PA2[:, 512:1024], PA3[:, 0:512], PA3[:, 512:1024]]
                vdone, addv_d = ffn_down(st2, H2, 1024, pf8, X3C, 1024, 0, hready)
                st2["pg_free"] = [("e_vector", addv_d[1]), ("e_vector", addv_d[3])]
                st2["pu_free"] = [("e_vector", addv_d[5]), ("e_vector", addv_d[7])]
                for s_ in range(NS):
                    fin = P.dma("scalar", lambda e, s_=s_, c=c:
                                e.dma_start(out=out[s_ * 128:(s_ + 1) * 128,
                                                    c * 1024:(c + 1) * 1024],
                                            in_=X3C[:, s_ * 1024:(s_ + 1) * 1024]),
                                "qout", wait=[("qout", fin),
                                              ("e_vector", addv_d[s_])])
                    st_out[s_] = fin
                outst = fin
            P.op("sync", None, wait=[("qout", fin)], inc=(("e_sync", 1),))
            P.flush()

    ctx.close()
    return nc


_NC = None


def _get_nc():
    global _NC
    if _NC is None:
        _NC = build_nc()
    return _NC


def _i8(a):
    a = np.ascontiguousarray(a)
    assert a.dtype == np.int32
    return np.ascontiguousarray(a.view(np.int8).reshape(a.shape + (4,))[..., 0])


def kernel(L0_attn_x, L0_attn_wq, L0_attn_wk_a, L0_attn_wk_b, L0_attn_wv_a,
           L0_attn_wv_b, L0_attn_wo, L0_ffn_w_gate, L0_ffn_w_up, L0_ffn_w_down):
    x = np.asarray(L0_attn_x, dtype=np.float32)
    A = np.ascontiguousarray
    wg = np.asarray(L0_ffn_w_gate, dtype=np.int32)
    wu = np.asarray(L0_ffn_w_up, dtype=np.int32)
    wd = np.asarray(L0_ffn_w_down, dtype=np.int32)
    shared = {
        "wq_n": A(np.asarray(L0_attn_wq, dtype=np.float32)),
        "wkb_n": A(np.asarray(L0_attn_wk_b, dtype=np.float32)),
        "wka_t": A(np.asarray(L0_attn_wk_a, dtype=np.float32).T),
        "wva_t": A(np.asarray(L0_attn_wv_a, dtype=np.float32).T),
        "wvb_t": A(np.asarray(L0_attn_wv_b, dtype=np.float32).T),
        "wo_t": A(np.asarray(L0_attn_wo, dtype=np.float32).T),
        "wgu_h": np.concatenate(
            [_i8(A(wg.reshape(NJ, 128, NS, 128).transpose(0, 3, 2, 1).reshape(NJ, 128, D))),
             _i8(A(wu.reshape(NJ, 128, NS, 128).transpose(0, 3, 2, 1).reshape(NJ, 128, D)))],
            axis=2),
        "wd_h": _i8(A(wd.T.reshape(NJ, 128, D))),
        "rt": _rope_rt(),
        "selc": _selc(),
    }
    in_maps = []
    for c in range(8):
        b, h = c // 2, c % 2
        im = dict(shared)
        im["xt"] = A(x[b, h * T:(h + 1) * T, :].T)
        in_maps.append(im)

    nc = _get_nc()
    res = run_bass_kernel_spmd(nc, in_maps, core_ids=list(range(8)))
    outp = np.empty((4, SEQ, D), dtype=np.float32)
    for c in range(8):
        b, h = c // 2, c % 2
        outp[b, h * T:(h + 1) * T, :] = res.results[c]["out"].T
    return outp


# version 23
# speedup vs baseline: 1.0188x; 1.0188x over previous
import contextlib
import numpy as np
import concourse.bass as bass
import concourse.mybir as mybir
from concourse.bass_utils import run_bass_kernel_spmd

F32 = mybir.dt.float32
BF16 = mybir.dt.bfloat16
I32 = mybir.dt.int32
I8 = mybir.dt.int8
AF = mybir.ActivationFunctionType
OP = mybir.AluOpType

D = 1024
T = 2048
SEQ = 4096
HF = 3072
LAT = 256
EPS = 1e-5
NS = D // 128
NJ = HF // 128
NP = 8
TCF = 1024
NCF = T // TCF
QCW = 512
NQC = T // QCW
C2 = 512
NC2 = T // C2
NKT = SEQ // 128

ENGINES = ("sync", "gpsimd", "scalar", "vector", "tensor")

TRUNC = None
FINE = None


class Planner:

    def __init__(self, nc, sems, ctx):
        self.nc = nc
        self.ctx = ctx
        self.sems = dict(sems)
        self.count = {k: 0 for k in self.sems}
        self.items = {e: [] for e in ENGINES}
        self.dma_owner = {}

    def _sem(self, name):
        if name not in self.sems:
            self.sems[name] = self.ctx.enter_context(self.nc.semaphore(name))
            self.count[name] = 0
        return self.sems[name]

    def op(self, eng, fn, wait=(), inc=()):
        w = [(self._sem(s), v) for s, v in wait if v > 0]
        i = []
        for s, n in inc:
            self._sem(s)
            self.count[s] += n
            i.append((self.sems[s], n))
        self.items[eng].append((w, fn, i))

    def dma(self, eng, fn, sem, wait=()):
        self._sem(sem)
        own = self.dma_owner.setdefault(sem, eng)
        assert own == eng, f"dma sem {sem} used from {eng} and {own}"
        self.op(eng, fn, wait=wait, inc=((sem, 16),))
        return self.count[sem]

    def step(self, eng, fn, wait=()):
        self.op(eng, fn, wait=wait, inc=((f"e_{eng}", 1),))
        return self.count[f"e_{eng}"]

    def v(self, sem):
        return self.count.get(sem, 0)

    def flush(self):
        target = self.count["bar"] + len(ENGINES)
        for e in ENGINES:
            w = [(self.sems[sname], self.count[sname])
                 for sname, own in self.dma_owner.items()
                 if own == e and self.count[sname] > 0]
            self.items[e].append((w, None, [(self.sems["bar"], 1)]))
            self.items[e].append(([(self.sems["bar"], target)], None, []))
        self.count["bar"] = target
        items = self.items
        self.items = {e: [] for e in ENGINES}

        def runner(engine, seq):
            last = {}
            for waits, fn, incs in seq:
                for h, v in waits:
                    if last.get(h.num, -1) < v:
                        engine.wait_ge(h, v)
                        last[h.num] = v
                inst = fn(engine) if fn is not None else None
                for h, n in incs:
                    if inst is not None:
                        inst = inst.then_inc(h, n)
                        inst = None if inst is None else inst
                    else:
                        engine.sem_inc(h, n)

        with self.nc.Block() as blk:
            for e in ENGINES:
                getattr(blk, e)(lambda eng, seq=items[e]: runner(eng, seq))


def _selc():
    import ml_dtypes
    out = np.zeros((16, NS * 128), dtype=np.float32)
    for dc in range(NS):
        for m in range(128):
            out[2 * dc + m // 64, dc * 128 + m] = 1.0
    return np.ascontiguousarray(out.astype(ml_dtypes.bfloat16))


def _rope_rt():
    inv = 1.0 / 10000.0 ** (np.arange(0, 64, 2, dtype=np.float64) / 64.0)
    out = np.zeros((NS, 128, 128), dtype=np.float32)
    for s in range(NS):
        R = np.eye(128, dtype=np.float64)
        for hh in (2 * s, 2 * s + 1):
            base = (hh - 2 * s) * 64
            for j in range(8):
                th = hh * inv[j]
                c, sn = np.cos(th), np.sin(th)
                R[base + 2 * j, base + 2 * j] = c
                R[base + 2 * j, base + 2 * j + 1] = -sn
                R[base + 2 * j + 1, base + 2 * j] = sn
                R[base + 2 * j + 1, base + 2 * j + 1] = c
        out[s] = R.T.astype(np.float32)
    return np.ascontiguousarray(out.reshape(NS * 128, 128))


def build_nc():
    nc = bass.Bass("TRN2", target_bir_lowering=False, debug=False)

    xin = nc.dram_tensor("xt", [D, T], F32, kind="ExternalInput").ap()
    wq_n = nc.dram_tensor("wq_n", [D, D], F32, kind="ExternalInput").ap()
    wkb_n = nc.dram_tensor("wkb_n", [D, LAT], F32, kind="ExternalInput").ap()
    wka_t = nc.dram_tensor("wka_t", [D, LAT], F32, kind="ExternalInput").ap()
    wva_t = nc.dram_tensor("wva_t", [D, LAT], F32, kind="ExternalInput").ap()
    wvb_t = nc.dram_tensor("wvb_t", [LAT, D], F32, kind="ExternalInput").ap()
    wo_t = nc.dram_tensor("wo_t", [D, D], F32, kind="ExternalInput").ap()
    wgu_h = nc.dram_tensor("wgu_h", [NJ, 128, 2 * D], I8, kind="ExternalInput").ap()
    wd_h = nc.dram_tensor("wd_h", [NJ, 128, D], I8, kind="ExternalInput").ap()
    rt_in = nc.dram_tensor("rt", [NS * 128, 128], F32, kind="ExternalInput").ap()
    selc_in = nc.dram_tensor("selc", [16, NS * 128], BF16, kind="ExternalInput").ap()
    out = nc.dram_tensor("out", [D, T], F32, kind="ExternalOutput").ap()

    q_hbm = nc.dram_tensor("q_hbm", [D, T], BF16, kind="Internal").ap()
    o_hbm = nc.dram_tensor("o_hbm", [D, T], BF16, kind="Internal").ap()
    x2_hbm = nc.dram_tensor("x2_hbm", [D, T], F32, kind="Internal").ap()
    kcin = nc.dram_tensor("kcin", [LAT, T], BF16, kind="Internal").ap()
    kcout = nc.dram_tensor("kcout", [2 * LAT, T], BF16, kind="Internal").ap()
    vcin = nc.dram_tensor("vcin", [LAT, T], BF16, kind="Internal").ap()
    vcout = nc.dram_tensor("vcout", [2 * LAT, T], BF16, kind="Internal").ap()
    k2_hbm = nc.dram_tensor("k2_hbm", [D, SEQ], BF16, kind="Internal").ap()
    oz_hbm = nc.dram_tensor("oz_hbm", [16, T], BF16, kind="Internal").ap()
    RG = [[0, 1], [2, 3], [4, 5], [6, 7]]

    _uid = [0]

    def un(tag):
        _uid[0] += 1
        return f"{tag}_{_uid[0]}"

    ctx = contextlib.ExitStack()
    sems = {}
    for name in ("e_tensor", "e_scalar", "e_vector",
                 "e_gpsimd", "e_sync", "cc", "bar"):
        sems[name] = ctx.enter_context(nc.semaphore(name))
    P = Planner(nc, sems, ctx)

    onescol = ctx.enter_context(nc.sbuf_tensor("onescol", [128, 1], BF16))
    onesrow = ctx.enter_context(nc.sbuf_tensor("onesrow", [128, 128], BF16))
    ones32 = ctx.enter_context(nc.sbuf_tensor("ones32", [128, 128], F32))
    xb = ctx.enter_context(nc.sbuf_tensor("xb", [128, 1024], BF16))
    xsq = ctx.enter_context(nc.sbuf_tensor("xsq", [128, 1024], BF16))
    rows = ctx.enter_context(nc.sbuf_tensor("rows", [1, 4 * 1024], F32))
    silu_sb = ctx.enter_context(nc.sbuf_tensor("silu_sb", [128, 4 * 512], F32))
    wgup = ctx.enter_context(nc.sbuf_tensor("wgup", [128, 2 * 2048], BF16))
    wdp = ctx.enter_context(nc.sbuf_tensor("wdp", [128, 2 * 1024], BF16))
    stg = ctx.enter_context(nc.sbuf_tensor("stg", [128, 4 * 512], BF16))
    fsb = ctx.enter_context(nc.sbuf_tensor("fsb", [128, 2 * 512], F32))
    WKBR = ctx.enter_context(nc.sbuf_tensor("WKBR", [128, 2 * 1024], BF16))
    WVB = ctx.enter_context(nc.sbuf_tensor("WVB", [128, 2 * 1024], BF16))
    lnf = ctx.enter_context(nc.sbuf_tensor("lnf", [128, 1024], F32))

    def row(i):
        return rows[:, i * 1024:(i + 1) * 1024]

    P.op("gpsimd", lambda g: g.memset(onescol[:], 1.0))
    P.op("gpsimd", lambda g: g.memset(onesrow[:], 1.0))
    ebias = ctx.enter_context(nc.sbuf_tensor("ebias", [1, 2], F32))
    P.op("gpsimd", lambda g: g.memset(ones32[:], 1.0))
    P.op("gpsimd", lambda g: g.memset(ebias[0:1, 0:1], float(EPS * 127.0 * 127.0)))
    P.op("gpsimd", lambda g: g.memset(ebias[0:1, 1:2], float(EPS)),
         inc=(("e_gpsimd", 1),))
    INIT = P.v("e_gpsimd")

    def ln(src, s_stride, s_col0, dst, d_stride, d_col0, ncols, a_scale, ps,
           dst_wait=None, src_ready=None, ret_applies=False):
        assert ncols % 1024 == 0
        state = {"xb_free": [0, 0], "ba_free": 0}
        lastv = P.v("e_vector")

        def vstep(fn, extra=()):
            state["lv"] = P.step("vector", fn,
                                 wait=[("e_vector", state["lv"])] + list(extra))
            return state["lv"]
        state["lv"] = lastv
        applies = []
        for c0 in range(0, ncols, 1024):
            sw0 = ("e_vector", P.v("e_vector"))
            for s in range(NS):
                sc = src[:, s * s_stride + s_col0 + c0: s * s_stride + s_col0 + c0 + 1024]
                w = [("e_tensor", state["xb_free"][0])]
                if src_ready is not None:
                    w += list(src_ready(c0 // 1024, s))
                P.step("scalar", lambda sc_, sc=sc:
                       sc_.activation(out=xb[:], in_=sc, func=AF.Copy), wait=w)
                sqv = P.step("scalar", lambda sc_, sc=sc:
                             sc_.activation(out=xsq[:], in_=sc, func=AF.Square))
                for hh in range(2):
                    P.step("tensor", lambda t, s=s, hh=hh:
                           t.matmul(ps["psx"][:, hh * 512:(hh + 1) * 512], onescol[:],
                                    xb[:, hh * 512:(hh + 1) * 512],
                                    start=(s == 0), stop=(s == NS - 1),
                                    skip_group_check=True),
                           wait=([("e_scalar", sqv), ("e_gpsimd", INIT), sw0]
                                 if hh == 0 else []))
                    P.step("tensor", lambda t, s=s, hh=hh:
                           t.matmul(ps["psx2"][:, hh * 512:(hh + 1) * 512], onescol[:],
                                    xsq[:, hh * 512:(hh + 1) * 512],
                                    start=(s == 0), stop=(s == NS - 1),
                                    skip_group_check=True))
                state["xb_free"][0] = P.v("e_tensor")
            tstats = P.v("e_tensor")
            vstep(lambda v: v.tensor_scalar_mul(row(0), ps["psx"][:], 1.0 / D),
                  extra=[("e_tensor", tstats)])
            vstep(lambda v: v.tensor_scalar_mul(row(2), ps["psx2"][:], 1.0 / D))
            vstep(lambda v: v.tensor_mul(row(1), row(0), row(0)))
            vv = vstep(lambda v: v.tensor_sub(row(2), row(2), row(1)))
            inv2 = 1.0 / (a_scale * a_scale)
            bsl = 0 if a_scale != 1.0 else 1
            sv = P.step("scalar", lambda sc_, bsl=bsl: sc_.activation(
                out=row(2), in_=row(2), func=AF.Sqrt,
                scale=float(inv2), bias=ebias[0:1, bsl:bsl + 1]),
                wait=[("e_vector", vv), ("e_gpsimd", INIT)])
            vstep(lambda v: v.reciprocal(row(3), row(2)), extra=[("e_scalar", sv)])
            vv = vstep(lambda v: v.scalar_tensor_tensor(row(1), row(0), -1.0, row(3),
                                                        OP.mult, OP.mult))
            for hh in range(2):
                P.step("tensor", lambda t, hh=hh:
                       t.matmul(ps["pba"][:, hh * 512:(hh + 1) * 512], ones32[0:1, :],
                                row(3)[:, hh * 512:(hh + 1) * 512],
                                start=True, stop=True, skip_group_check=True),
                       wait=([("e_vector", vv), ("e_vector", state["ba_free"]),
                              ("e_gpsimd", INIT)] if hh == 0 else []))
            for hh in range(2):
                P.step("tensor", lambda t, hh=hh:
                       t.matmul(ps["pbb"][:, hh * 512:(hh + 1) * 512], ones32[0:1, :],
                                row(1)[:, hh * 512:(hh + 1) * 512],
                                start=True, stop=True, skip_group_check=True))
            tb = P.v("e_tensor")
            applies = []
            for s in range(NS):
                sc = src[:, s * s_stride + s_col0 + c0: s * s_stride + s_col0 + c0 + 1024]
                dc = dst[:, s * d_stride + d_col0 + c0: s * d_stride + d_col0 + c0 + 1024]
                fpar = s % 2
                extra = [("e_tensor", tb)]
                if s == 0 and dst_wait is not None:
                    extra.append(dst_wait)
                vstep(lambda v, sc=sc:
                      v.tensor_mul(lnf[:], sc, ps["pba"][:]),
                      extra=extra)
                lastv = vstep(lambda v, dc=dc:
                              v.tensor_add(dc, lnf[:], ps["pbb"][:]))
                applies.append(lastv)
            state["ba_free"] = lastv
        if ret_applies:
            return lastv, applies
        return lastv

    def ffn_gateup(st, xlnbuf, xl_stride, ncols, hbuf, pg, pu, wait_ready,
                   xl_col0=0):
        nu = ncols // 512
        for j in range(NJ):
            par = j % 2
            lu = P.dma("gpsimd", lambda g, j=j, par=par:
                       g.dma_start(out=wgup[:, par * 2048:(par + 1) * 2048],
                                   in_=wgu_h[j]),
                       f"qw{par}", wait=[st["wg_free"][par]])
            wr = wait_ready if isinstance(wait_ready, list) else [wait_ready] * NS
            for u in range(nu):
                for dc in range(NS):
                    w = [(f"qw{par}", lu), ("e_vector", wr[dc])]
                    if dc == 0 and u == 0:
                        w.append(st["pg_free"][par])
                    P.step("tensor", lambda t, par=par, dc=dc, u=u:
                           t.matmul(pg[par][:, u * 512:(u + 1) * 512],
                                    wgup[:, par * 2048 + dc * 128: par * 2048 + (dc + 1) * 128],
                                    xlnbuf[:, dc * xl_stride + xl_col0 + u * 512:
                                            dc * xl_stride + xl_col0 + (u + 1) * 512],
                                    start=(dc == 0), stop=(dc == NS - 1),
                                    skip_group_check=True), wait=w)
                tg = P.v("e_tensor")
                for dc in range(NS):
                    w = []
                    if dc == 0 and u == 0:
                        w.append(st["pu_free"][par])
                    P.step("tensor", lambda t, par=par, dc=dc, u=u:
                           t.matmul(pu[par][:, u * 512:(u + 1) * 512],
                                    wgup[:, par * 2048 + 1024 + dc * 128: par * 2048 + 1024 + (dc + 1) * 128],
                                    xlnbuf[:, dc * xl_stride + xl_col0 + u * 512:
                                            dc * xl_stride + xl_col0 + (u + 1) * 512],
                                    start=(dc == 0), stop=(dc == NS - 1),
                                    skip_group_check=True), wait=w)
                tu = P.v("e_tensor")
                if u == nu - 1:
                    st["wg_free"][par] = ("e_tensor", tu)
                slot = (j % 2) * 2 + u
                ss = P.step("scalar", lambda s, par=par, u=u, slot=slot:
                            s.activation(out=silu_sb[:, slot * 512:(slot + 1) * 512],
                                         in_=pg[par][:, u * 512:(u + 1) * 512],
                                         func=AF.Silu),
                            wait=[("e_tensor", tg), st["silu_free"][slot]])
                if u == nu - 1:
                    st["pg_free"][par] = ("e_scalar", ss)
                mv = P.step("vector", lambda v, j=j, par=par, u=u, slot=slot:
                            v.tensor_mul(hbuf[:, j * ncols + u * 512: j * ncols + (u + 1) * 512],
                                         silu_sb[:, slot * 512:(slot + 1) * 512],
                                         pu[par][:, u * 512:(u + 1) * 512]),
                            wait=[("e_scalar", ss), ("e_tensor", tu)])
                st["silu_free"][slot] = ("e_vector", mv)
                if u == nu - 1:
                    st["pu_free"][par] = ("e_vector", mv)
        return P.v("e_vector")

    def ffn_down(st, hbuf, ncols, pf, resid, r_stride, r_col0, hready):
        wd_src = wd_h
        nu = ncols // 512
        ng = NS // len(pf)
        lastv = 0
        addv_d = [0] * NS
        for u in range(nu):
            for g_ in range(ng):
                for j in range(NJ):
                    par = j % 2
                    ld = P.dma("gpsimd", lambda g, j=j, par=par, wd_src=wd_src:
                               g.dma_start(out=wdp[:, par * 1024:(par + 1) * 1024],
                                           in_=wd_src[j]),
                               f"qd{par}", wait=[st["wd_free"][par]])
                    for k_ in range(len(pf)):
                        i = g_ * len(pf) + k_
                        w = [(f"qd{par}", ld)]
                        if j == 0:
                            w += [("e_vector", hready), st["pf_free"]]
                        P.step("tensor", lambda t, par=par, i=i, k_=k_, j=j, u=u:
                               t.matmul(pf[k_][:],
                                        wdp[:, par * 1024 + i * 128: par * 1024 + (i + 1) * 128],
                                        hbuf[:, j * ncols + u * 512: j * ncols + (u + 1) * 512],
                                        start=(j == 0), stop=(j == NJ - 1),
                                        skip_group_check=True), wait=w)
                    st["wd_free"][par] = ("e_tensor", P.v("e_tensor"))
                tdown = P.v("e_tensor")
                for k_ in range(len(pf)):
                    i = g_ * len(pf) + k_
                    par = i % 2
                    ss = P.step("scalar", lambda s, k_=k_, par=par:
                                s.activation(out=fsb[:, par * 512:(par + 1) * 512],
                                             in_=pf[k_][:], func=AF.Copy, scale=0.5 / 127.0),
                                wait=[("e_tensor", tdown), st["fsb_free"][par]])
                    rsl = resid[:, i * r_stride + r_col0 + u * 512:
                                i * r_stride + r_col0 + (u + 1) * 512]
                    lastv = P.step("vector", lambda v, rsl=rsl, par=par:
                                   v.tensor_add(rsl, rsl, fsb[:, par * 512:(par + 1) * 512]),
                                   wait=[("e_scalar", ss)])
                    addv_d[i] = lastv
                    st["fsb_free"][par] = ("e_vector", lastv)
                st["pf_free"] = ("e_vector", lastv)
        return lastv, addv_d

    def new_ffn_state():
        z = ("e_tensor", 0)
        zv = ("e_vector", 0)
        zs = ("e_scalar", 0)
        return dict(wg_free=[z, z], pg_free=[zs, zs], pu_free=[zv, zv],
                    wd_free=[z, z], pf_free=zv, fsb_free=[zv, zv],
                    silu_free=[zv, zv, zv, zv], wb_free=[0, 0], wbd_free=[0, 0])

    PF_NAMES = [f"pf{i}" for i in range(8)]

    def pf_scope(es, tag):
        return [es.enter_context(nc.psum_tensor(un(f"{tag}{i}"), [128, 512], F32))
                for i in range(8)]

    with nc.sbuf_tensor("X", [128, NS * T], F32) as X, \
         nc.sbuf_tensor("XLN", [128, NS * T], BF16) as XLN:

        xld = 0
        for s in range(NS):
            xld = P.dma("sync", lambda e, s=s:
                        e.dma_start(out=X[:, s * T:(s + 1) * T],
                                    in_=xin[s * 128:(s + 1) * 128, :]), "qx")

        def dump_X():
            vv = P.v("e_vector")
            last = 0
            for s_ in range(NS):
                last = P.dma("sync", lambda e, s_=s_:
                             e.dma_start(out=out[s_ * 128:(s_ + 1) * 128, :],
                                         in_=X[:, s_ * T:(s_ + 1) * T]), "qx",
                             wait=[("e_vector", vv), ("qx", xld)])
            P.op("sync", None, wait=[("qx", last)], inc=(("e_sync", 1),))
            P.flush()

        if FINE == "load":
            dump_X()
            return nc

        with nc.sbuf_tensor("H", [128, NJ * TCF], BF16) as H, \
             nc.psum_tensor("PB0", [128, 1024], F32) as PB0, \
             nc.psum_tensor("PB1", [128, 1024], F32) as PB1, \
             nc.psum_tensor("PB2", [128, 1024], F32) as PB2, \
             nc.psum_tensor("PB3", [128, 1024], F32) as PB3:
            st1 = new_ffn_state()

            def ln1(tc, first=False):
                if first:
                    P.op("vector", None, wait=[("qx", xld)], inc=(("e_vector", 1),))
                ps = dict(psx=PB2[0:1, :], psx2=PB3[0:1, :], pba=PB2, pbb=PB3)
                return ln(X, T, tc * TCF, XLN, T, tc * TCF, TCF, 1.0 / 127.0, ps,
                          dst_wait=("e_tensor", P.v("e_tensor")),
                          src_ready=(lambda ci, s: [("qx", xld)]) if first else None,
                          ret_applies=True)

            pf8 = [PB0[:, 0:512], PB0[:, 512:1024], PB1[:, 0:512], PB1[:, 512:1024],
                   PB2[:, 0:512], PB2[:, 512:1024], PB3[:, 0:512], PB3[:, 512:1024]]
            xl1ready, ap1 = ln1(0, first=True)
            for tc in range(NCF):
                hready = ffn_gateup(st1, XLN, T, TCF, H, [PB0, PB1],
                                    [PB2, PB3], ap1,
                                    xl_col0=tc * TCF)
                if tc + 1 < NCF:
                    xl1ready, ap1 = ln1(tc + 1)
                    st1["pf_free"] = ("e_vector", xl1ready)
                else:
                    st1["pf_free"] = st1["pg_free"][1]
                vdone, _ = ffn_down(st1, H, TCF, pf8, X, T, tc * TCF, hready)
                st1["pg_free"] = [("e_vector", vdone), ("e_vector", vdone)]
                st1["pu_free"] = [("e_vector", vdone), ("e_vector", vdone)]
            P.flush()

        with nc.sbuf_tensor("WQR", [128, NS * 1024], BF16) as WQR:
            with nc.sbuf_tensor("WQN", [128, NS * 1024], BF16) as WQN, \
                 nc.sbuf_tensor("WKBN", [128, NS * 256], BF16) as WKBN, \
                 nc.sbuf_tensor("RT", [128, NS * 128], BF16) as RT, \
                 nc.psum_tensor("prf0", [128, 128], F32) as prf0, \
                 nc.psum_tensor("prf1", [128, 128], F32) as prf1:
                prf = [prf0, prf1]
                strip_ld = [None] * NS
                pv3 = [0, 0, 0]
                for s in range(NS):
                    pv3[0] = P.dma("gpsimd", lambda g, s=s:
                                   g.dma_start(out=RT[:, s * 128:(s + 1) * 128],
                                               in_=rt_in[s * 128:(s + 1) * 128, :]),
                                   "qrf0",
                                   wait=([("qrf0", pv3[0])] if s else []))
                    pv3[1] = P.dma("gpsimd", lambda g, s=s:
                                   g.dma_start(out=WQN[:, s * 1024:(s + 1) * 1024],
                                               in_=wq_n[s * 128:(s + 1) * 128, :]),
                                   "qrf1",
                                   wait=([("qrf1", pv3[1])] if s else []))
                    pv3[2] = P.dma("gpsimd", lambda g, s=s:
                                   g.dma_start(out=WKBN[:, s * 256:(s + 1) * 256],
                                               in_=wkb_n[s * 128:(s + 1) * 128, :]),
                                   "qrf2",
                                   wait=([("qrf2", pv3[2])] if s else []))
                    strip_ld[s] = [("qrf0", pv3[0]), ("qrf1", pv3[1]),
                                   ("qrf2", pv3[2])]
                evfree = [0, 0]
                k = 0
                for fs in range(NS):
                    for dc in range(NS):
                        par = k % 2
                        w = [("e_vector", evfree[par])]
                        if dc == 0:
                            w += strip_ld[fs]
                        tmm = P.step("tensor", lambda t, dc=dc, fs=fs, par=par:
                                     t.matmul(prf[par][:],
                                              WQN[:, fs * 1024 + dc * 128: fs * 1024 + (dc + 1) * 128],
                                              RT[:, fs * 128:(fs + 1) * 128],
                                              start=True, stop=True,
                                              skip_group_check=True),
                                     wait=w)
                        evfree[par] = P.step(
                            "vector", lambda v, dc=dc, fs=fs, par=par:
                            v.tensor_copy(WQR[:, dc * 1024 + fs * 128: dc * 1024 + (fs + 1) * 128],
                                          prf[par][:]),
                            wait=[("e_tensor", tmm)])
                        k += 1
                    for lc in range(2):
                        par = k % 2
                        tmm = P.step("tensor", lambda t, lc=lc, fs=fs, par=par:
                                     t.matmul(prf[par][:],
                                              WKBN[:, fs * 256 + lc * 128: fs * 256 + (lc + 1) * 128],
                                              RT[:, fs * 128:(fs + 1) * 128],
                                              start=True, stop=True,
                                              skip_group_check=True),
                                     wait=[("e_vector", evfree[par])])
                        evfree[par] = P.step(
                            "vector", lambda v, lc=lc, fs=fs, par=par:
                            v.tensor_copy(WKBR[:, lc * 1024 + fs * 128: lc * 1024 + (fs + 1) * 128],
                                          prf[par][:]),
                            wait=[("e_tensor", tmm)])
                        k += 1
                P.flush()

            with nc.sbuf_tensor("WKA", [128, NS * 256], BF16) as WKA, \
                 nc.sbuf_tensor("WVA", [128, NS * 256], BF16) as WVA, \
                 nc.psum_tensor("pp0", [128, 512], F32) as pp0, \
                 nc.psum_tensor("pp1", [128, 512], F32) as pp1, \
                 nc.psum_tensor("pp2", [128, 512], F32) as pp2, \
                 nc.psum_tensor("pp3", [128, 512], F32) as pp3, \
                 nc.psum_tensor("pbaB", [128, 1024], F32) as pba, \
                 nc.psum_tensor("pbbB", [128, 1024], F32) as pbb:
                pp = [pp0, pp1, pp2, pp3]
                wl = 0
                for s in range(NS):
                    P.dma("gpsimd", lambda g, s=s:
                          g.dma_start(out=WKA[:, s * 256:(s + 1) * 256],
                                      in_=wka_t[s * 128:(s + 1) * 128, :]), "qaw")
                    wl = P.dma("gpsimd", lambda g, s=s:
                               g.dma_start(out=WVA[:, s * 256:(s + 1) * 256],
                                           in_=wva_t[s * 128:(s + 1) * 128, :]), "qaw")
                for lc in range(2):
                    wl = P.dma("gpsimd", lambda g, lc=lc:
                               g.dma_start(out=WVB[:, lc * 1024:(lc + 1) * 1024],
                                           in_=wvb_t[lc * 128:(lc + 1) * 128, :]), "qaw")

                kslot = [("e_scalar", 0)] * 4
                stg_last = [("qqw0", 0), ("qqw1", 0), ("qqw2", 0), ("qqw3", 0)]
                kidx = [0]

                def proj(lhs_fn, n_out_blocks, rhs_fn, nacc, evac_fn, wait0=()):
                    for ob in range(n_out_blocks):
                        sl = kidx[0] % 4
                        kidx[0] += 1
                        for a in range(nacc):
                            w = list(wait0)
                            if a == 0:
                                w.append(kslot[sl])
                            P.step("tensor", lambda t, ob=ob, a=a, sl=sl:
                                   t.matmul(pp[sl][:], lhs_fn(ob, a), rhs_fn(ob, a),
                                            start=(a == 0), stop=(a == nacc - 1),
                                            skip_group_check=True), wait=w)
                        kslot[sl] = evac_fn(ob, pp[sl], P.v("e_tensor"))

                def hbm_evac(dst, sem_pfx):
                    def evac(ob_dst_cols, psum, tmm, par):
                        ev = P.step("scalar", lambda s_, psum=psum, par=par:
                                    s_.activation(out=stg[:, par * 512:(par + 1) * 512],
                                                  in_=psum[:], func=AF.Copy),
                                    wait=[("e_tensor", tmm), stg_last[par]])
                        stg_last[par] = (f"{sem_pfx}{par}", P.dma(
                            "sync", lambda e, ob_dst_cols=ob_dst_cols, par=par:
                            e.dma_start(out=ob_dst_cols,
                                        in_=stg[:, par * 512:(par + 1) * 512]),
                            f"{sem_pfx}{par}", wait=[("e_scalar", ev)]))
                        return ("e_scalar", ev)
                    return evac

                xl2 = 0
                for tc in range(NCF):
                    ps = dict(psx=pba[0:1, :], psx2=pbb[0:1, :], pba=pba, pbb=pbb)
                    xl2 = ln(X, T, tc * TCF, XLN, T, tc * TCF, TCF, 1.0, ps,
                             dst_wait=None)
                    for s in range(NS):
                        P.dma("sync", lambda e, s=s, tc=tc:
                              e.dma_start(out=x2_hbm[s * 128:(s + 1) * 128,
                                                     tc * TCF:(tc + 1) * TCF],
                                          in_=X[:, s * T + tc * TCF: s * T + (tc + 1) * TCF]),
                              "qx2", wait=[("e_vector", xl2)])

                kev = hbm_evac(kcin, "qkc")

                def klevac(ob, psum, tmm):
                    ls, u = divmod(ob, 4)
                    return kev(kcin[ls * 128:(ls + 1) * 128, u * 512:(u + 1) * 512],
                               psum, tmm, ob % 4)

                proj(lambda ob, a: WKA[:, a * 256 + (ob // 4) * 128: a * 256 + (ob // 4) * 128 + 128],
                     2 * 4,
                     lambda ob, a: XLN[:, a * T + (ob % 4) * 512: a * T + (ob % 4) * 512 + 512],
                     NS, klevac, wait0=[("e_vector", xl2), ("qaw", wl)])
                P.op("gpsimd", lambda g:
                     g.collective_compute("AllGather", OP.bypass, replica_groups=RG,
                                          ins=[kcin], outs=[kcout]),
                     wait=[(f"qkc{i}", P.v(f"qkc{i}")) for i in range(4)],
                     inc=(("cc", 1),))

                vev = hbm_evac(vcin, "qvc")

                def vlevac(ob, psum, tmm):
                    ls, u = divmod(ob, 4)
                    return vev(vcin[ls * 128:(ls + 1) * 128, u * 512:(u + 1) * 512],
                               psum, tmm, ob % 4)

                proj(lambda ob, a: WVA[:, a * 256 + (ob // 4) * 128: a * 256 + (ob // 4) * 128 + 128],
                     2 * 4,
                     lambda ob, a: XLN[:, a * T + (ob % 4) * 512: a * T + (ob % 4) * 512 + 512],
                     NS, vlevac)
                P.op("gpsimd", lambda g:
                     g.collective_compute("AllGather", OP.bypass, replica_groups=RG,
                                          ins=[vcin], outs=[vcout]),
                     wait=[(f"qvc{i}", P.v(f"qvc{i}")) for i in range(4)],
                     inc=(("cc", 1),))

                def qevac(ob, psum, tmm):
                    fs, u = divmod(ob, 4)
                    par = ob % 4
                    ev = P.step("scalar", lambda s_, psum=psum, par=par:
                                s_.activation(out=stg[:, par * 512:(par + 1) * 512],
                                              in_=psum[:], func=AF.Copy),
                                wait=[("e_tensor", tmm), stg_last[par]])
                    stg_last[par] = (f"qqw{par}", P.dma(
                        "sync", lambda e, fs=fs, u=u, par=par:
                        e.dma_start(out=q_hbm[fs * 128:(fs + 1) * 128,
                                              u * 512:(u + 1) * 512],
                                    in_=stg[:, par * 512:(par + 1) * 512]),
                        f"qqw{par}", wait=[("e_scalar", ev)]))
                    return ("e_scalar", ev)

                proj(lambda ob, a: WQR[:, a * 1024 + (ob // 4) * 128: a * 1024 + (ob // 4) * 128 + 128],
                     NS * 4,
                     lambda ob, a: XLN[:, a * T + (ob % 4) * 512: a * T + (ob % 4) * 512 + 512],
                     NS, qevac)
                P.flush()

    with nc.sbuf_tensor("WOT", [128, NS * 1024], BF16) as WOT:
        wotld = 0
        for s in range(NS):
            wotld = P.dma("gpsimd", lambda g, s=s:
                          g.dma_start(out=WOT[:, s * 1024:(s + 1) * 1024],
                                      in_=wo_t[s * 128:(s + 1) * 128, :]), "qwo")

        with nc.sbuf_tensor("VSB", [128, NKT * 1056], BF16) as VSB, \
             nc.sbuf_tensor("KB", [128, 2 * SEQ], BF16) as KB, \
             nc.sbuf_tensor("QB", [128, 2 * QCW], BF16) as QB, \
             nc.sbuf_tensor("PT", [128, 4 * 1024], BF16) as PT, \
             nc.sbuf_tensor("ONST", [128, 4 * 512], BF16) as ONST, \
             nc.sbuf_tensor("KLT", [128, 4 * 512], BF16) as KLT, \
             nc.sbuf_tensor("VLT", [128, 2 * 2048], BF16) as VLT, \
             nc.psum_tensor("S00", [128, 1024], F32) as S00, \
             nc.psum_tensor("S10", [128, 1024], F32) as S10, \
             nc.psum_tensor("OT00", [128, 512], F32) as OT00, \
             nc.psum_tensor("OT01", [128, 512], F32) as OT01, \
             nc.psum_tensor("OT10", [128, 512], F32) as OT10, \
             nc.psum_tensor("OT11", [128, 512], F32) as OT11:
            SS = [S00, S10]
            OT = [[OT00, OT01], [OT10, OT11]]

            genp = [S00[:, 0:512], S00[:, 512:1024], S10[:, 0:512], S10[:, 512:1024]]
            gfree = [0, 0, 0, 0]
            stg_last2 = [("qk2w0", 0), ("qk2w1", 0), ("qk2w2", 0), ("qk2w3", 0)]
            gi = 0
            kl_use = [0, 0]
            kl_ld = [0, 0]
            for tch in range(NS):
                par = tch % 2
                slot = tch // 4
                for lc in range(2):
                    kl_ld[par] = P.dma(
                        "sync", lambda e, slot=slot, lc=lc, tch=tch, par=par:
                        e.dma_start(out=KLT[:, (par * 2 + lc) * 512:(par * 2 + lc + 1) * 512],
                                    in_=kcout[slot * LAT + lc * 128: slot * LAT + (lc + 1) * 128,
                                              (tch % 4) * 512:(tch % 4) * 512 + 512]),
                        f"qlk{par}",
                        wait=([("cc", 1), ("e_tensor", kl_use[par])] if lc == 0 else []))
                for fs in range(NS):
                    sl = gi % 4
                    gi += 1
                    for lc in range(2):
                        w = [(f"qlk{par}", kl_ld[par])]
                        if lc == 0:
                            w.append(("e_vector", gfree[sl]))
                        P.step("tensor", lambda t, fs=fs, lc=lc, sl=sl, par=par:
                               t.matmul(genp[sl][:],
                                        WKBR[:, lc * 1024 + fs * 128: lc * 1024 + (fs + 1) * 128],
                                        KLT[:, (par * 2 + lc) * 512:(par * 2 + lc + 1) * 512],
                                        start=(lc == 0), stop=(lc == 1),
                                        skip_group_check=True), wait=w)
                    kl_use[par] = P.v("e_tensor")
                    tmm = kl_use[par]
                    spar = sl
                    ev = P.step("vector", lambda v, spar=spar, sl=sl:
                                v.tensor_copy(stg[:, spar * 512:(spar + 1) * 512], genp[sl][:]),
                                wait=[("e_tensor", tmm), stg_last2[spar]])
                    gfree[sl] = ev
                    stg_last2[spar] = (f"qk2w{spar}", P.dma(
                        "sync", lambda e, fs=fs, tch=tch, spar=spar:
                        e.dma_start(out=k2_hbm[fs * 128:(fs + 1) * 128,
                                               tch * 512:(tch + 1) * 512],
                                    in_=stg[:, spar * 512:(spar + 1) * 512]),
                        f"qk2w{spar}", wait=[("e_vector", ev)]))
            vmset = P.step("vector", lambda v: v.memset(VSB[:], 1.0))
            vl_use = 0
            vl_ld = 0
            for kt in range(NKT):
                slot = kt // 16
                if kt % 16 == 0:
                    for lc in range(2):
                        vl_ld = P.dma(
                            "sync", lambda e, slot=slot, lc=lc:
                            e.dma_start(out=VLT[:, lc * 2048:(lc + 1) * 2048],
                                        in_=vcout[slot * LAT + lc * 128:
                                                  slot * LAT + (lc + 1) * 128, :]),
                            "qlv0",
                            wait=([("cc", 2), ("e_tensor", vl_use)] if lc == 0 else []))
                for fc in range(2):
                    sl = gi % 4
                    gi += 1
                    for lc in range(2):
                        w = [("qlv0", vl_ld)]
                        if lc == 0:
                            w.append(("e_vector", gfree[sl]))
                        P.step("tensor", lambda t, kt=kt, fc=fc, lc=lc, sl=sl:
                               t.matmul(genp[sl][:],
                                        VLT[:, lc * 2048 + (kt % 16) * 128:
                                            lc * 2048 + (kt % 16) * 128 + 128],
                                        WVB[:, lc * 1024 + fc * 512: lc * 1024 + (fc + 1) * 512],
                                        start=(lc == 0), stop=(lc == 1),
                                        skip_group_check=True), wait=w)
                    vl_use = P.v("e_tensor")
                    tmm = vl_use
                    dst3 = VSB[:, kt * 1056 + fc * 8 * 66: kt * 1056 + (fc * 8 + 8) * 66] \
                        .rearrange("p (h f) -> p h f", f=66)[:, :, 0:64]
                    src3 = genp[sl].rearrange("p (h f) -> p h f", f=64)
                    gfree[sl] = P.step("vector", lambda v, dst3=dst3, src3=src3:
                                       v.tensor_copy(dst3, src3),
                                       wait=[("e_tensor", tmm), ("e_vector", vmset)])
            vgen_done = P.v("e_vector")

            kld = [0, 0]
            kuse = [0, 0]
            qld = [0, 0]
            quse = [0, 0]
            exp_done = [0, 0]
            pt_use = [0, 0]
            ot_evac = [0, 0]
            sB_free = 0
            onst_dma = [0, 0]

            k2done = [P.v(f"qk2w{i}") for i in range(4)]

            def load_k(p, par):
                w = [("e_tensor", kuse[par])]
                w += [(f"qk2w{i}", k2done[i]) for i in range(4)]
                kld[par] = P.dma("sync", lambda e, p=p, par=par:
                                 e.dma_start(out=KB[:, par * SEQ:(par + 1) * SEQ],
                                             in_=k2_hbm[p * 128:(p + 1) * 128, :]),
                                 f"qk{par}", wait=w)

            def load_q(p, qc, par):
                qld[par] = P.dma("sync", lambda e, p=p, qc=qc, par=par:
                                 e.dma_start(out=QB[:, par * QCW:(par + 1) * QCW],
                                             in_=q_hbm[p * 128:(p + 1) * 128,
                                                       qc * QCW:(qc + 1) * QCW]),
                                 f"qq{par}", wait=[("e_tensor", quse[par])])

            load_k(0, 0)
            load_q(0, 0, 0)
            it = 0
            NIT = NP * NQC

            def emit_pv(p, kt, opar):
                spar = kt % 2
                w = [("e_scalar", exp_done[spar]), ("e_vector", vgen_done)]
                if kt == 0:
                    w.append(("e_vector", ot_evac[opar]))
                first = [True]
                for h in range(2):
                    OTP = OT[h][opar]
                    base = kt * 1056 + (2 * p + h) * 66
                    P.step("tensor", lambda t, OTP=OTP, base=base, kt=kt,
                           h=h, spar=spar:
                           t.matmul(OTP[0:65, :],
                                    VSB[:, base: base + 65],
                                    PT[:, spar * 1024 + h * 512: spar * 1024 + (h + 1) * 512],
                                    start=(kt == 0), stop=(kt == NKT - 1),
                                    skip_group_check=True),
                           wait=(w if first[0] else ()))
                    first[0] = False
                pt_use[spar] = P.v("e_tensor")

            for p in range(NP):
                kpar = p % 2
                if p + 1 < NP:
                    load_k(p + 1, (p + 1) % 2)
                for qc in range(NQC):
                    qpar = it % 2
                    opar = it % 2
                    if it + 1 < NIT:
                        np_, nqc = divmod(it + 1, NQC)
                        load_q(np_, nqc, (it + 1) % 2)
                    for kt in range(NKT):
                        spar = kt % 2
                        for h in range(2):
                            rs = 0 if h == 0 else 64
                            w = []
                            if h == 0:
                                w = [(f"qk{kpar}", kld[kpar]),
                                     (f"qq{qpar}", qld[qpar]),
                                     ("e_scalar", exp_done[spar])]
                                if kt <= 1:
                                    w.append(("e_vector", vgen_done))
                            P.step("tensor", lambda t, rs=rs, kt=kt, h=h,
                                   spar=spar, kpar=kpar, qpar=qpar:
                                   t.matmul(SS[spar][:, h * 512:(h + 1) * 512],
                                            KB[rs:rs + 64,
                                               kpar * SEQ + kt * 128: kpar * SEQ + (kt + 1) * 128],
                                            QB[rs:rs + 64, qpar * QCW:(qpar + 1) * QCW],
                                            start=True, stop=True,
                                            tile_position=(rs, 0),
                                            skip_group_check=True), wait=w)
                        smm = P.v("e_tensor")
                        quse[qpar] = smm
                        kuse[kpar] = smm
                        exp_done[spar] = P.step(
                            "scalar", lambda sc, spar=spar:
                            sc.activation(out=PT[:, spar * 1024:(spar + 1) * 1024],
                                          in_=SS[spar][:], func=AF.Exp, scale=0.125),
                            wait=[("e_tensor", smm), ("e_tensor", pt_use[spar])])
                        if kt > 0:
                            emit_pv(p, kt - 1, opar)
                    emit_pv(p, NKT - 1, opar)
                    pvdone = P.v("e_tensor")
                    n0 = P.step("vector", lambda v, opar=opar:
                                v.tensor_copy(ONST[0:65, (2 * opar) * 512:(2 * opar + 1) * 512],
                                              OT[0][opar][0:65, :]),
                                wait=[("e_tensor", pvdone),
                                      (f"qo{opar}", onst_dma[opar])])
                    n1 = P.step("vector", lambda v, opar=opar:
                                v.tensor_copy(ONST[0:65, (2 * opar + 1) * 512:(2 * opar + 2) * 512],
                                              OT[1][opar][0:65, :]))
                    ot_evac[opar] = n1
                    P.dma("sync", lambda e, p=p, qc=qc, opar=opar:
                          e.dma_start(out=o_hbm[p * 128:p * 128 + 64,
                                                qc * QCW:(qc + 1) * QCW],
                                      in_=ONST[0:64, (2 * opar) * 512:(2 * opar + 1) * 512]),
                          f"qo{opar}", wait=[("e_vector", n1)])
                    P.dma("sync", lambda e, p=p, qc=qc, opar=opar:
                          e.dma_start(out=o_hbm[p * 128 + 64:(p + 1) * 128,
                                                qc * QCW:(qc + 1) * QCW],
                                      in_=ONST[0:64, (2 * opar + 1) * 512:(2 * opar + 2) * 512]),
                          f"qo{opar}")
                    P.dma("sync", lambda e, p=p, qc=qc, opar=opar:
                          e.dma_start(out=oz_hbm[2 * p:2 * p + 1, qc * QCW:(qc + 1) * QCW],
                                      in_=ONST[64:65, (2 * opar) * 512:(2 * opar + 1) * 512]),
                          f"qo{opar}")
                    onst_dma[opar] = P.dma(
                        "sync", lambda e, p=p, qc=qc, opar=opar:
                        e.dma_start(out=oz_hbm[2 * p + 1:2 * p + 2, qc * QCW:(qc + 1) * QCW],
                                    in_=ONST[64:65, (2 * opar + 1) * 512:(2 * opar + 2) * 512]),
                        f"qo{opar}", wait=[("e_vector", n1)])
                    it += 1
            oz_ld = P.dma("gpsimd", lambda g:
                          g.dma_start(out=silu_sb[0:16, 0:2048], in_=oz_hbm),
                          "qoz",
                          wait=[("qo0", P.v("qo0")), ("qo1", P.v("qo1"))])
            ozv = P.step("vector", lambda v:
                         v.reciprocal(silu_sb[0:16, 0:2048],
                                      silu_sb[0:16, 0:2048]),
                         wait=[("qoz", oz_ld)])
            ozv = P.step("vector", lambda v:
                         v.tensor_copy(stg[0:16, 0:2048],
                                       silu_sb[0:16, 0:2048]),
                         wait=[("e_vector", ozv)])
            ozv_box = ozv
            P.flush()

        if TRUNC == 3:
            with nc.sbuf_tensor("odmp", [128, T], BF16) as odmp, \
                 nc.sbuf_tensor("odmp2", [128, T], F32) as odmp2:
                for s_ in range(NS):
                    l1 = P.dma("sync", lambda e, s_=s_:
                               e.dma_start(out=odmp[:], in_=o_hbm[s_ * 128:(s_ + 1) * 128, :]),
                               "qdump")
                    c1 = P.step("vector", lambda v: v.tensor_copy(odmp2[:], odmp[:]),
                                wait=[("qdump", l1)])
                    l2 = P.dma("sync", lambda e, s_=s_:
                               e.dma_start(out=out[s_ * 128:(s_ + 1) * 128, :], in_=odmp2[:]),
                               "qdump", wait=[("e_vector", c1)])
                P.op("sync", None, wait=[("qdump", l2)], inc=(("e_sync", 1),))
                P.flush()
            return nc

        with nc.sbuf_tensor("OBUF", [128, NS * 1024], BF16) as OBUF, \
             nc.sbuf_tensor("SELC", [16, NS * 128], BF16) as SELC, \
             nc.sbuf_tensor("ZBT", [128, 2 * 512], BF16) as ZBT, \
             nc.sbuf_tensor("X3C", [128, NS * 1024], F32) as X3C, \
             nc.sbuf_tensor("XLN3", [128, NS * 1024], BF16) as XLN3, \
             nc.sbuf_tensor("H2", [128, NJ * 1024], BF16) as H2, \
             nc.psum_tensor("PA0", [128, 1024], F32) as PA0, \
             nc.psum_tensor("PA1", [128, 1024], F32) as PA1, \
             nc.psum_tensor("PA2", [128, 1024], F32) as PA2, \
             nc.psum_tensor("PA3", [128, 1024], F32) as PA3:
            PA = [PA0, PA1, PA2, PA3]
            st2 = new_ffn_state()
            st2["pf_free"] = ("e_scalar", 0)
            wo_last = 0
            outst = 0
            fin = 0

            OZRv = stg[0:16, 0:2048]
            ozv = ozv_box
            st2["silu_free"] = [("e_vector", ozv)] * 4
            selld = P.dma("gpsimd", lambda g:
                          g.dma_start(out=SELC[:], in_=selc_in), "qwo")
            nrm_prev = [0, 0]

            x2q = [0]
            st_out = [0] * NS

            def wo_ln3(c):
                nonlocal wo_last
                o_ld = 0
                for half in range(2):
                    cc = 2 * c + half
                    osrc = o_hbm[:, cc * 512:(cc + 1) * 512].rearrange(
                        "(s p) t -> p s t", p=128)
                    odst = OBUF[:, half * NS * 512:(half + 1) * NS * 512].rearrange(
                        "p (s t) -> p s t", t=512)
                    o_ld = P.dma("sync", lambda e, osrc=osrc, odst=odst:
                                 e.dma_start(out=odst, in_=osrc), "qc0",
                                 wait=[("e_tensor", wo_last)])
                x2l = [0] * NS
                for s_ in range(NS):
                    w = [("qxr0", x2q[0])]
                    if c == 1:
                        w += [("qout", st_out[s_]), ("e_scalar", P.v("e_scalar"))]
                    x2q[0] = P.dma("sync", lambda e, s_=s_, c=c:
                                   e.dma_start(out=X3C[:, s_ * 1024:(s_ + 1) * 1024],
                                               in_=x2_hbm[s_ * 128:(s_ + 1) * 128,
                                                          c * 1024:(c + 1) * 1024]),
                                   "qxr0", wait=w)
                    x2l[s_] = x2q[0]
                nrm = {}
                for half in range(2):
                    cc = 2 * c + half
                    hb = half * NS * 512
                    for dc in range(NS):
                        k2 = dc % 2
                        bmm = P.step("tensor", lambda t, dc=dc, cc=cc, k2=k2:
                                     t.matmul(PA[2 + k2][0:128, 0:512],
                                              SELC[:, dc * 128:(dc + 1) * 128],
                                              OZRv[:, cc * 512:(cc + 1) * 512],
                                              start=True, stop=True,
                                              skip_group_check=True),
                                     wait=[("e_vector", ozv), ("qwo", selld),
                                           ("e_vector", nrm_prev[k2]),
                                           st2["pu_free"][k2]])
                        zc = P.step("vector", lambda v, k2=k2:
                                    v.tensor_copy(ZBT[:, k2 * 512:(k2 + 1) * 512],
                                                  PA[2 + k2][0:128, 0:512]),
                                    wait=[("e_tensor", bmm)])
                        nrm[(half, dc)] = P.step(
                            "vector", lambda v, dc=dc, hb=hb, k2=k2:
                            v.tensor_mul(OBUF[:, hb + dc * 512: hb + (dc + 1) * 512],
                                         OBUF[:, hb + dc * 512: hb + (dc + 1) * 512],
                                         ZBT[:, k2 * 512:(k2 + 1) * 512]),
                            wait=[("e_vector", zc), ("qc0", o_ld)])
                        nrm_prev[k2] = nrm[(half, dc)]
                st2["pu_free"] = [("e_vector", nrm_prev[0]), ("e_vector", nrm_prev[1])]
                lastev = 0
                rot = [PA0[:, 0:512], PA0[:, 512:1024], PA1[:, 0:512], PA1[:, 512:1024]]
                rot_free = [st2["pg_free"][0], st2["pg_free"][0],
                            st2["pg_free"][1], st2["pg_free"][1]]
                addv = [0] * NS
                for half in range(2):
                    hb = half * NS * 512
                    for fs in range(NS):
                        k4 = (half * NS + fs) % 4
                        for dc in range(NS):
                            w = [("qc0", o_ld), ("qwo", wotld),
                                 ("e_vector", nrm[(half, dc)])]
                            if dc == 0:
                                w.append(rot_free[k4])
                            P.step("tensor", lambda t, fs=fs, dc=dc, k4=k4, hb=hb:
                                   t.matmul(rot[k4][:],
                                            WOT[:, dc * 1024 + fs * 128: dc * 1024 + (fs + 1) * 128],
                                            OBUF[:, hb + dc * 512: hb + (dc + 1) * 512],
                                            start=(dc == 0), stop=(dc == NS - 1),
                                            skip_group_check=True), wait=w)
                        tmm = P.v("e_tensor")
                        x3sl = X3C[:, fs * 1024 + half * 512: fs * 1024 + (half + 1) * 512]
                        lastev = P.step("vector", lambda v, x3sl=x3sl, k4=k4:
                                        v.tensor_add(x3sl, x3sl, rot[k4][:]),
                                        wait=[("e_tensor", tmm), ("qxr0", x2l[fs])])
                        rot_free[k4] = ("e_vector", lastev)
                        addv[fs] = lastev
                wo_last = P.v("e_tensor")
                ps = dict(psx=PA2[0:1, :], psx2=PA3[0:1, :], pba=PA2, pbb=PA3)
                xl3r, ap3 = ln(X3C, 1024, 0, XLN3, 1024, 0, 1024, 1.0 / 127.0, ps,
                               dst_wait=("e_tensor", P.v("e_tensor")),
                               src_ready=lambda ci, s: [("e_vector", addv[s])],
                               ret_applies=True)
                st2["pg_free"] = [("e_vector", lastev), ("e_vector", lastev)]
                st2["pu_free"] = [("e_vector", xl3r), ("e_vector", xl3r)]
                return xl3r, ap3

            for c in range(2):
                xl3ready, ap3 = wo_ln3(c)
                hready = ffn_gateup(st2, XLN3, 1024, 1024, H2, [PA0, PA1],
                                    [PA2, PA3], ap3)
                st2["pf_free"] = st2["pg_free"][1]
                pf8 = [PA0[:, 0:512], PA0[:, 512:1024], PA1[:, 0:512], PA1[:, 512:1024],
                       PA2[:, 0:512], PA2[:, 512:1024], PA3[:, 0:512], PA3[:, 512:1024]]
                vdone, addv_d = ffn_down(st2, H2, 1024, pf8, X3C, 1024, 0, hready)
                st2["pg_free"] = [("e_vector", vdone), ("e_vector", vdone)]
                st2["pu_free"] = [("e_vector", vdone), ("e_vector", vdone)]
                for s_ in range(NS):
                    fin = P.dma("scalar", lambda e, s_=s_, c=c:
                                e.dma_start(out=out[s_ * 128:(s_ + 1) * 128,
                                                    c * 1024:(c + 1) * 1024],
                                            in_=X3C[:, s_ * 1024:(s_ + 1) * 1024]),
                                "qout", wait=[("qout", fin),
                                              ("e_vector", addv_d[s_])])
                    st_out[s_] = fin
                outst = fin
            P.op("sync", None, wait=[("qout", fin)], inc=(("e_sync", 1),))
            P.flush()

    ctx.close()
    return nc


_NC = None


def _get_nc():
    global _NC
    if _NC is None:
        _NC = build_nc()
    return _NC


def _i8(a):
    a = np.ascontiguousarray(a)
    assert a.dtype == np.int32
    return np.ascontiguousarray(a.view(np.int8).reshape(a.shape + (4,))[..., 0])


def kernel(L0_attn_x, L0_attn_wq, L0_attn_wk_a, L0_attn_wk_b, L0_attn_wv_a,
           L0_attn_wv_b, L0_attn_wo, L0_ffn_w_gate, L0_ffn_w_up, L0_ffn_w_down):
    x = np.asarray(L0_attn_x, dtype=np.float32)
    A = np.ascontiguousarray
    wg = np.asarray(L0_ffn_w_gate, dtype=np.int32)
    wu = np.asarray(L0_ffn_w_up, dtype=np.int32)
    wd = np.asarray(L0_ffn_w_down, dtype=np.int32)
    shared = {
        "wq_n": A(np.asarray(L0_attn_wq, dtype=np.float32)),
        "wkb_n": A(np.asarray(L0_attn_wk_b, dtype=np.float32)),
        "wka_t": A(np.asarray(L0_attn_wk_a, dtype=np.float32).T),
        "wva_t": A(np.asarray(L0_attn_wv_a, dtype=np.float32).T),
        "wvb_t": A(np.asarray(L0_attn_wv_b, dtype=np.float32).T),
        "wo_t": A(np.asarray(L0_attn_wo, dtype=np.float32).T),
        "wgu_h": np.concatenate(
            [_i8(A(wg.reshape(NJ, 128, NS, 128).transpose(0, 3, 2, 1).reshape(NJ, 128, D))),
             _i8(A(wu.reshape(NJ, 128, NS, 128).transpose(0, 3, 2, 1).reshape(NJ, 128, D)))],
            axis=2),
        "wd_h": _i8(A(wd.T.reshape(NJ, 128, D))),
        "rt": _rope_rt(),
        "selc": _selc(),
    }
    in_maps = []
    for c in range(8):
        b, h = c // 2, c % 2
        im = dict(shared)
        im["xt"] = A(x[b, h * T:(h + 1) * T, :].T)
        in_maps.append(im)

    nc = _get_nc()
    res = run_bass_kernel_spmd(nc, in_maps, core_ids=list(range(8)))
    outp = np.empty((4, SEQ, D), dtype=np.float32)
    for c in range(8):
        b, h = c // 2, c % 2
        outp[b, h * T:(h + 1) * T, :] = res.results[c]["out"].T
    return outp
